# version 08
# baseline (speedup 1.0000x reference)
import numpy as np
import ml_dtypes
from contextlib import ExitStack

import concourse.bass as bass
import concourse.mybir as mybir
from concourse.bass_utils import run_bass_kernel_spmd

F32 = mybir.dt.float32
BF16 = mybir.dt.bfloat16
AF = mybir.ActivationFunctionType
ALU = mybir.AluOpType
AX = mybir.AxisListType.X

D = 1024
NCH = 8
SEQ = 4096
TOWN = 2048
NT = 32
NQG = 4
QW = 512
NKC = 32
NHP = 8
EPS = 1e-5
ASC = 0.125

ENGS = ("sync", "gpsimd", "tensor", "vector", "scalar")
ESEM = {"tensor": "ts", "vector": "vs", "scalar": "ss", "gpsimd_c": "gs"}
QSEMS = {"sync": ("dSa", "dSb"), "gpsimd": ("dGa", "dGb")}


class P:
    def __init__(self, nc, stack):
        self.nc = nc
        self.prog = {k: [] for k in ENGS}
        self.sems = {}
        self.cnt = {}
        self.qflip = {"sync": 0, "gpsimd": 0}
        for s in ("dSa", "dSb", "dGa", "dGb", "ts", "vs", "ss", "gs"):
            self.sems[s] = stack.enter_context(nc.semaphore(s))
            self.cnt[s] = 0

    def step(self, eng, need, fn, n=1, drain=False):
        need = list(need)
        if eng in QSEMS:
            sem = QSEMS[eng][self.qflip[eng]]
            self.qflip[eng] ^= 1
            inc = 16
            need.append((self.sems[sem], self.cnt[sem]))
        else:
            sem = ESEM[eng]
            inc = 1
        self.cnt[sem] += inc * n
        h = self.sems[sem]
        tok = (h, self.cnt[sem])
        need = tuple(t for t in need if t is not None)

        def cl(e, need=need, fn=fn, h=h, inc=inc, n=n, eng=eng, drain=drain):
            if drain:
                e.drain()
            for (hh, v) in need:
                if v > 0 and not (hh is h and eng not in QSEMS):
                    e.wait_ge(hh, v)
            r = fn(e)
            rs = r if isinstance(r, list) else [r]
            assert len(rs) == n, f"step expected {n} instrs, got {len(rs)}"
            for ri in rs:
                ri.then_inc(h, inc)

        self.prog["gpsimd" if eng == "gpsimd_c" else eng].append(cl)
        return tok

    def qtok(self, eng):
        return [(self.sems[q], self.cnt[q]) for q in QSEMS[eng]]

    def emit(self, block):
        for eng in ENGS:
            fns = self.prog[eng]
            if not fns:
                continue

            def body(e, fns=list(fns)):
                for f in fns:
                    f(e)

            getattr(block, eng)(body)
        self.prog = {k: [] for k in ENGS}


def build_nc(probe=False):
    nc = bass.Bass("TRN2", target_bir_lowering=False, debug=False)

    xfull = nc.dram_tensor("xfull", [SEQ, D], F32, kind="ExternalInput").ap()
    w_in = {n: nc.dram_tensor(n, [D, D], F32, kind="ExternalInput").ap()
            for n in ("wq", "wk", "wv", "wp", "wrt", "wkt", "wvt")}
    wmod = nc.dram_tensor("wmod", [D, 6 * D], F32, kind="ExternalInput").ap()
    cfm_d = nc.dram_tensor("c_fm", [128, 8], F32, kind="ExternalInput").ap()
    mur_d = nc.dram_tensor("mu_r_fm", [128, 8], F32, kind="ExternalInput").ap()
    muk_d = nc.dram_tensor("mu_k_fm", [128, 8], F32, kind="ExternalInput").ap()
    psv_d = nc.dram_tensor("ps_col", [128, 1], F32, kind="ExternalInput").ap()
    id_d = nc.dram_tensor("identb", [128, 128], BF16, kind="ExternalInput").ap()
    idf_d = nc.dram_tensor("identf", [128, 128], F32, kind="ExternalInput").ap()
    y = nc.dram_tensor("y", [TOWN, D], F32, kind="ExternalOutput").ap()

    scrk = "ExternalOutput" if probe else "Internal"
    k_scr = nc.dram_tensor("k_scr", [128, NCH, SEQ], BF16, kind=scrk).ap()
    vt_scr = nc.dram_tensor("vt_scr", [NHP, 128, NKC, 130], BF16, kind=scrk).ap()
    q_scr = nc.dram_tensor("q_scr", [128, NCH, TOWN], BF16, kind=scrk).ap()
    w_scr = {n: nc.dram_tensor(n + "_scr", [128, NCH, D], BF16, kind=scrk).ap()
             for n in ("wp", "wrt", "wkt", "wvt")}
    zr_d = nc.dram_tensor("zr_d", [1, 8192], F32, kind="Internal").ap()
    rr_d = nc.dram_tensor("rr_d", [1, 8192], F32, kind="Internal").ap()
    if probe:
        rows_dump = nc.dram_tensor("rows_dump", [12, D], F32, kind="ExternalOutput").ap()
        fm_dump = nc.dram_tensor("fm_dump", [128, 64], F32, kind="ExternalOutput").ap()
        ofm_dump = nc.dram_tensor("ofm_dump", [NQG, 128, NCH, QW], BF16,
                                  kind="ExternalOutput").ap()
        fm2_dump = nc.dram_tensor("fm2_dump", [128, 64], F32, kind="ExternalOutput").ap()

    wview = {n: ap.rearrange("(i p) n -> p i n", p=128) for n, ap in w_in.items()}
    wmodv = wmod.rearrange("(i p) n -> p i n", p=128)
    x_t = xfull.rearrange("(t p) d -> t p d", p=128)
    y_t = y.rearrange("(t p) d -> t p d", p=128)
    zr_sp = zr_d.rearrange("o (p f) -> (o p) f", p=128)
    rr_sp = rr_d.rearrange("o (p f) -> (o p) f", p=128)

    with ExitStack() as big:
        sbuf = lambda n, s, d: big.enter_context(nc.sbuf_tensor(n, s, d))
        psum = lambda n, s, d: big.enter_context(nc.psum_tensor(n, s, d))

        negC = sbuf("negC", [128, 16], F32)
        mfc = sbuf("mfc", [128, NT], F32)
        rfc = sbuf("rfc", [128, NT], F32)
        ga1t = sbuf("ga1t", [128, D], F32)
        ga0t = sbuf("ga0t", [128, D], F32)
        muR = sbuf("muR", [128, 8], F32)
        muK = sbuf("muK", [128, 8], F32)
        psv = sbuf("psv", [128, 1], F32)
        zterm = sbuf("zterm", [128, 8], BF16)
        bq_f = sbuf("bq_f", [128, 8], F32)
        bk_f = sbuf("bk_f", [128, 8], F32)
        bv_f = sbuf("bv_f", [128, 8], F32)
        brh_f = sbuf("brh_f", [128, 8], F32)
        bk2_f = sbuf("bk2_f", [128, 8], F32)
        identb = sbuf("identb_s", [128, 128], BF16)
        ones128 = sbuf("ones128", [1, 128], F32)
        one11 = sbuf("one11", [1, 1], F32)
        ones64p = sbuf("ones64p", [65, 64], F32)
        bd_ones = sbuf("bd_ones", [128, 128], BF16)
        mq2 = sbuf("mq2", [128, 8], F32)
        mk2 = sbuf("mk2", [128, 8], F32)
        negcp = sbuf("negcp", [128, 8], F32)
        negcpB = sbuf("negcpB", [1, 8], F32)
        qh = sbuf("qh", [128, 8], BF16)
        one11b = sbuf("one11b", [1, 1], BF16)

        S_A = psum("S_A", [128, 1024], F32)
        S_B = psum("S_B", [128, 1024], F32)
        PO_A = psum("PO_A", [128, 512], F32)
        PO_B = psum("PO_B", [128, 512], F32)
        PSC = psum("PSC", [128, 512], F32)
        PSCB = psum("PSCB", [128, 1024], BF16)

        blk = big.enter_context(nc.Block())
        pr = P(nc, big)
        st = pr.step

        with ExitStack() as seg:
            asb = lambda n, s, d: seg.enter_context(nc.sbuf_tensor(n, s, d))
            wqs = asb("wqs", [128, NCH, D], BF16)
            wks = asb("wks", [128, NCH, D], BF16)
            wvs = asb("wvs", [128, NCH, D], BF16)
            wfold = asb("wfold", [128, NCH, D], BF16)
            wmgf = asb("wmgf", [128, NCH, D], F32)
            cfm = asb("cfm", [128, 8], F32)
            tnh = asb("tnh", [128, 8], F32)
            silu_f = asb("silu_f", [128, 8], F32)
            idf = asb("idf", [128, 128], F32)
            rowbuf = asb("rowbuf", [1, D], F32)
            sa_fm = asb("sa_fmv", [128, 8], F32)
            ca_fm = asb("ca_fm", [128, 8], F32)
            sbv_fm = asb("sbv_fm", [128, 8], F32)
            cb_fm = asb("cb_fm", [128, 8], F32)
            ga_fm = asb("ga_fm", [128, 8], F32)
            gb_fm = asb("gb_fm", [128, 8], F32)
            t0_fm = asb("t0_fm", [128, 8], F32)
            t1_fm = asb("t1_fm", [128, 8], F32)
            ga1_fm = asb("ga1_fm", [128, 8], F32)
            ga0_fm = asb("ga0_fm", [128, 8], F32)
            g05_fm = asb("g05_fm", [128, 8], F32)
            sa_bf = asb("sa_bf", [128, 8], BF16)
            sb_bf = asb("sb_bf", [128, 8], BF16)
            opca_fm = asb("opca_fm", [128, 8], F32)
            opcb_fm = asb("opcb_fm", [128, 8], F32)
            zf = asb("zf", [128, 8], F32)
            om = asb("om", [128, 1], F32)
            gabc = asb("gabc", [128, D], BF16)
            gb05bc = asb("gb05bc", [128, D], BF16)
            xt = asb("xt", [128, 2, D], F32)
            bn12 = asb("bn12", [128, 12], F32)
            mv2 = asb("mv2", [128, 2], F32)
            wvar = asb("wvar", [128, 1], F32)
            qvar = asb("qvar", [128, 1], F32)
            xhf = asb("xhf", [128, D], F32)
            xhb = asb("xhb", [128, D], BF16)
            xfm = asb("xfm", [128, 2, NCH, 512], BF16)
            kst = asb("kst", [128, NCH, 512], BF16)
            vst = asb("vst", [128, NCH, 512], BF16)
            qst = asb("qst", [128, NCH, 512], BF16)
            vtst = asb("vtst", [128, 4, NHP, 130], BF16)
            sqb = asb("sqb", [128, 4096], BF16)
            red8 = asb("red8", [128, 8], F32)

            def f(e):
                e.memset(ones128[:], 1.0)
                e.memset(one11[:], 1.0)
                e.memset(one11b[:], 1.0)
                e.memset(ones64p[64:65, :], 1.0)
                e.memset(bd_ones[:], 0.0)
                e.drain()
                e.memset(bd_ones[0:64, 0:64], 1.0)
                e.memset(bd_ones[64:128, 64:128], 1.0)
                e.memset(mq2[:], 0.0)
                e.memset(mk2[:], 0.0)
                e.memset(vtst[:, :, :, 64:65], 1.0)
                return e.memset(vtst[:, :, :, 129:130], 1.0)
            v_consts = st("vector", [], f, drain=True)

            def f(e):
                return [e.dma_start(out=cfm[:], in_=cfm_d[:, :]),
                        e.dma_start(out=muR[:], in_=mur_d[:, :]),
                        e.dma_start(out=muK[:], in_=muk_d[:, :]),
                        e.dma_start(out=psv[:], in_=psv_d[:, :]),
                        e.dma_start(out=idf[:], in_=idf_d[:, :]),
                        e.dma_start(out=identb[:], in_=id_d[:, :])]
            d_small = st("sync", [], f, n=6)

            def f(e):
                return [e.dma_start(out=wqs[:], in_=wview["wq"][:, :, :]),
                        e.dma_start(out=wks[:], in_=wview["wk"][:, :, :]),
                        e.dma_start(out=wvs[:], in_=wview["wv"][:, :, :])]
            d_wqkv = st("gpsimd", [], f, n=3)

            s_tnh = st("scalar", [d_small],
                       lambda e: e.activation(tnh[:], cfm[:], AF.Tanh, scale=0.5),
                       drain=True)

            def f(e):
                e.tensor_mul(silu_f[:], cfm[:], tnh[:])
                e.drain()
                e.tensor_add(silu_f[:], silu_f[:], cfm[:])
                e.drain()
                return e.tensor_scalar_mul(silu_f[:], silu_f[:], 0.5)
            v_silu = st("vector", [s_tnh], f, drain=True)

            prev_ev = None
            prev_ld = None
            for dst_fm, c0 in ((sa_fm, 0), (ca_fm, 1024), (ga_fm, 2048),
                               (sbv_fm, 3072), (cb_fm, 4096), (gb_fm, 5120)):
                d_f = st("sync", [prev_ev],
                         lambda e, c0=c0: e.dma_start(out=wmgf[:],
                                                      in_=wmodv[:, :, c0:c0 + 1024]))

                def f(e):
                    for j in range(NCH):
                        for i in range(NCH):
                            r = e.matmul(PSC[:, j:j + 1], wmgf[:, i, 128 * j:128 * (j + 1)],
                                         silu_f[:, i:i + 1],
                                         start=(i == 0), stop=(i == NCH - 1))
                    return r
                t_g = st("tensor", [d_f, v_silu, prev_ev], f)
                prev_ev = st("vector", [t_g],
                             lambda e, dst_fm=dst_fm: e.tensor_copy(dst_fm[:, 0:8],
                                                                    PSC[:, 0:8]),
                             drain=True)
            v_mod = prev_ev

            def f(e):
                e.activation(t0_fm[:], ga_fm[:], AF.Sign)
                return e.activation(t1_fm[:], ga_fm[:], AF.Abs)
            s_sg1 = st("scalar", [v_mod], f, drain=True)

            def f(e):
                e.tensor_scalar_max(t1_fm[:], t1_fm[:], 0.25)
                e.drain()
                return e.tensor_mul(ga_fm[:], t0_fm[:], t1_fm[:])
            v_ga = st("vector", [s_sg1], f, drain=True)

            def f(e):
                e.activation(t0_fm[:], gb_fm[:], AF.Sign)
                return e.activation(t1_fm[:], gb_fm[:], AF.Abs)
            s_sg2 = st("scalar", [v_ga], f, drain=True)

            def f(e):
                e.tensor_scalar_max(t1_fm[:], t1_fm[:], 0.25)
                e.drain()
                e.tensor_mul(gb_fm[:], t0_fm[:], t1_fm[:])
                e.drain()
                e.tensor_scalar_mul(g05_fm[:], gb_fm[:], 0.5)
                e.tensor_scalar_add(opca_fm[:], ca_fm[:], 1.0)
                e.tensor_scalar_add(opcb_fm[:], cb_fm[:], 1.0)
                e.drain()
                e.tensor_mul(ga1_fm[:], ga_fm[:], opca_fm[:])
                e.tensor_mul(ga0_fm[:], ga_fm[:], sa_fm[:])
                e.tensor_copy(sa_bf[:], sa_fm[:])
                e.tensor_copy(sb_bf[:], sbv_fm[:])
                e.reciprocal(zf[:], opcb_fm[:])
                e.drain()
                e.tensor_mul(zf[:], zf[:], sbv_fm[:])
                e.drain()
                e.tensor_scalar_mul(zf[:], zf[:], -1.0)
                e.tensor_scalar(om[:], psv[:], -1.0, 1.0, ALU.mult, ALU.add)
                e.drain()
                return e.tensor_scalar_mul(zterm[:], zf[:], om[:, 0:1])
            v_zterm = st("vector", [s_sg2, d_small], f, drain=True)
            v_fmmath = v_zterm

            def bcast(v_fm, dst, need):
                def f(e, v_fm=v_fm):
                    for i in range(NCH):
                        r = e.matmul(S_A[0:1, 128 * i:128 * (i + 1)], v_fm[:, i:i + 1],
                                     idf[:, :], start=True, stop=True)
                    return r
                t_r = st("tensor", list(need) + [v_fmmath, d_small], f)
                v_row = st("vector", [t_r],
                           lambda e: e.tensor_copy(rowbuf[0:1, :], S_A[0:1, :]),
                           drain=True)

                def f(e):
                    for half in range(2):
                        r = e.matmul(S_B[:, 512 * half:512 * (half + 1)], ones128[0:1, :],
                                     rowbuf[0:1, 512 * half:512 * (half + 1)],
                                     start=True, stop=True)
                    return r
                t_b = st("tensor", [v_row, v_consts], f)
                return st("vector", [t_b],
                          lambda e, dst=dst: e.tensor_copy(dst[:], S_B[:, :]),
                          drain=True)

            v_bc1 = bcast(ga_fm, gabc, [])
            v_bc2 = bcast(g05_fm, gb05bc, [v_bc1])
            v_bc3 = bcast(ga1_fm, ga1t, [v_bc2])
            v_bc4 = bcast(ga0_fm, ga0t, [v_bc3])

            def bias_fm(wsb, vec_bf, dst, need, halve=False):
                def f(e, wsb=wsb, vec_bf=vec_bf):
                    for j in range(NCH):
                        for i in range(NCH):
                            r = e.matmul(PSC[:, j:j + 1], wsb[:, i, 128 * j:128 * (j + 1)],
                                         vec_bf[:, i:i + 1],
                                         start=(i == 0), stop=(i == NCH - 1))
                    return r
                t_b = st("tensor", list(need) + [v_fmmath], f)

                def f(e, dst=dst, halve=halve):
                    e.tensor_copy(dst[:, 0:8], PSC[:, 0:8])
                    e.drain()
                    if halve:
                        return e.tensor_scalar_mul(dst[:], dst[:], 0.5)
                    return e.tensor_scalar_mul(dst[:], dst[:], 1.0)
                return st("vector", [t_b], f, drain=True)

            vb_q = bias_fm(wqs, sa_bf, bq_f, [d_wqkv, v_bc4])
            vb_k = bias_fm(wks, sa_bf, bk_f, [vb_q])
            vb_v = bias_fm(wvs, sa_bf, bv_f, [vb_k])

            def f(e):
                for wsb in (wqs, wks, wvs):
                    for i in range(NCH):
                        r = e.tensor_scalar_mul(wsb[:, i, :], wsb[:, i, :],
                                                opca_fm[:, i:i + 1])
                return r
            v_wqkv = st("vector", [vb_v], f, drain=True)

            prev_tok = v_wqkv
            for nm in ("wp", "wrt", "wkt", "wvt"):
                d_wf = st("gpsimd", [prev_tok],
                          lambda e, nm=nm: e.dma_start(out=wfold[:],
                                                       in_=wview[nm][:, :, :]))
                vbias = None
                if nm == "wrt":
                    vbias = bias_fm(wfold, sb_bf, brh_f, [d_wf], halve=True)
                elif nm == "wkt":
                    vbias = bias_fm(wfold, sb_bf, bk2_f, [d_wf])
                fold_need = [d_wf] if vbias is None else [vbias]

                def f(e, nm=nm):
                    if nm == "wp":
                        for i in range(NCH):
                            r = e.tensor_mul(wfold[:, i, :], wfold[:, i, :], gabc[:])
                    elif nm == "wvt":
                        for i in range(NCH):
                            r = e.tensor_mul(wfold[:, i, :], wfold[:, i, :], gb05bc[:])
                    else:
                        for i in range(NCH):
                            r = e.tensor_scalar_mul(wfold[:, i, :], wfold[:, i, :],
                                                    opcb_fm[:, i:i + 1])
                    return r
                extra = [v_bc2] if nm in ("wp", "wvt") else [v_fmmath]
                v_fold = st("vector", fold_need + extra, f, drain=True)
                prev_tok = st("gpsimd", [v_fold],
                              lambda e, nm=nm: e.dma_start(out=w_scr[nm][:, :, :],
                                                           in_=wfold[:]))
            d_wspill = prev_tok

            if probe:
                def f2(e):
                    return [e.dma_start(out=fm2_dump[:, 0:8], in_=silu_f[:]),
                            e.dma_start(out=fm2_dump[:, 8:16], in_=sa_fm[:]),
                            e.dma_start(out=fm2_dump[:, 16:24], in_=ca_fm[:]),
                            e.dma_start(out=fm2_dump[:, 24:32], in_=sbv_fm[:]),
                            e.dma_start(out=fm2_dump[:, 32:40], in_=cb_fm[:]),
                            e.dma_start(out=fm2_dump[:, 40:48], in_=ga_fm[:]),
                            e.dma_start(out=fm2_dump[:, 48:56], in_=gb_fm[:])]
                st("gpsimd", [v_zterm], f2, n=7)

                def f(e):
                    return [e.dma_start(out=fm_dump[:, 0:8], in_=opca_fm[:]),
                            e.dma_start(out=fm_dump[:, 8:16], in_=opcb_fm[:]),
                            e.dma_start(out=fm_dump[:, 16:24], in_=bq_f[:]),
                            e.dma_start(out=fm_dump[:, 24:32], in_=brh_f[:]),
                            e.dma_start(out=fm_dump[:, 32:40], in_=bk2_f[:]),
                            e.dma_start(out=fm_dump[:, 40:48], in_=zf[:]),
                            e.dma_start(out=rows_dump[0:1, :], in_=rowbuf[:, :]),
                            e.dma_start(out=rows_dump[1:2, 0:8], in_=ga_fm[0:1, :]),
                            e.dma_start(out=rows_dump[2:3, :], in_=ga1t[0:1, :])]
                st("gpsimd", [v_zterm, vb_v, d_wspill, v_bc4], f, n=9)

            v_evtr = [None] * NT
            t_xfm_user = [None, None]
            creg_user = [None] * 4
            tcm_sq_user = None
            d_kspill = d_vtspill = d_qspill = None
            v_cred = None
            s_rstd_prev = None
            t_tr_prev = None

            for t in range(NT):
                buf = t % 2
                grp, t4 = t // 4, t % 4
                gbuf = grp % 2

                d_x = st("sync", [v_evtr[t - 2] if t >= 2 else None],
                         lambda e, t=t, buf=buf: e.dma_start(out=xt[:, buf, :],
                                                             in_=x_t[t, :, :]))

                def f(e, t=t, buf=buf):
                    e.bn_stats(bn12[:, 0:6], xt[:, buf, 0:512])
                    e.bn_stats(bn12[:, 6:12], xt[:, buf, 512:1024])
                    e.drain()
                    e.bn_aggr(mv2[:], bn12[:].rearrange("p (g s) -> p g s", g=2))
                    e.drain()
                    e.tensor_copy(mfc[:, t:t + 1], mv2[:, 0:1])
                    e.tensor_scalar_add(wvar[:], mv2[:, 1:2], EPS)
                    e.drain()
                    return e.reciprocal(qvar[:], wvar[:])
                v_bn = st("vector", [d_x, v_consts, s_rstd_prev], f, drain=True)

                s_rstd = st("scalar", [v_bn],
                            lambda e, t=t: e.activation(rfc[:, t:t + 1], qvar[:], AF.Sqrt))
                s_rstd_prev = s_rstd

                def f(e, t=t, buf=buf):
                    e.tensor_scalar(xhf[:], xt[:, buf, :], mfc[:, t:t + 1], None,
                                    ALU.subtract)
                    e.drain()
                    return e.tensor_scalar_mul(xhb[:], xhf[:], rfc[:, t:t + 1])
                v_xh = st("vector", [s_rstd, t_tr_prev], f, drain=True)

                def f(e):
                    for j in range(NCH):
                        r = e.transpose(PSCB[:, 128 * j:128 * (j + 1)],
                                        xhb[:, 128 * j:128 * (j + 1)], identb[:])
                    return r
                t_tr = st("tensor", [v_xh, v_evtr[t - 1] if t >= 1 else None], f)
                t_tr_prev = t_tr

                v_evtr[t] = st("vector", [t_tr, t_xfm_user[gbuf]],
                               lambda e, gbuf=gbuf, t4=t4: e.tensor_copy(
                                   xfm[:, gbuf, :, 128 * t4:128 * (t4 + 1)],
                                   PSCB[:, :].rearrange("p (i s) -> p i s", i=8)))

                if t4 != 3:
                    continue

                g0 = 512 * grp
                v_grp = v_evtr[t]
                own = grp < 4
                last_mm = None

                def proj(wsb, bias, stage, need):
                    nonlocal last_mm
                    toks = []
                    mms = []
                    for j in range(NCH):
                        po = (PO_A, PO_B)[j % 2]

                        def f(e, wsb=wsb, j=j, po=po, gbuf=gbuf):
                            for i in range(NCH):
                                r = e.matmul(po[:, :], wsb[:, i, 128 * j:128 * (j + 1)],
                                             xfm[:, gbuf, i, :],
                                             start=(i == 0), stop=(i == NCH - 1))
                            return r
                        t_mm = st("tensor", list(need) + [v_grp] +
                                  ([toks[j - 2]] if j >= 2 else []), f)
                        mms.append(t_mm)
                        toks.append(st("scalar", [t_mm],
                                       lambda e, j=j, po=po, stage=stage, bias=bias:
                                       e.activation(stage[:, j, :], po[:, :], AF.Identity,
                                                    bias=bias[:, j:j + 1], scale=1.0),
                                       drain=(j == 0)))
                    last_mm = mms[-1]
                    return toks

                ktoks = proj(wks, bk_f, kst, [v_wqkv, d_kspill, v_cred])
                d_kspill = st("gpsimd", ktoks,
                              lambda e, g0=g0: e.dma_start(
                                  out=k_scr[:, :, g0:g0 + 512], in_=kst[:]))
                s_sqF = st("scalar", [ktoks[-1], v_cred, tcm_sq_user],
                           lambda e: e.activation(sqb[:, :], kst[:, :, :], AF.Square),
                           drain=True)
                cred_toks = []
                for hp in range(NHP):
                    reg = (S_A, S_A, S_B, S_B)[hp % 4]
                    c0 = 512 * ((hp % 4) % 2)

                    def f(e, reg=reg, c0=c0, hp=hp):
                        return e.matmul(reg[:, c0:c0 + 512], bd_ones[:, :],
                                        sqb[:, 512 * hp:512 * (hp + 1)],
                                        start=True, stop=True)
                    t_cm = st("tensor", [s_sqF, creg_user[hp % 4]], f)
                    tcm_sq_user = t_cm
                    creg_user[hp % 4] = st("vector", [t_cm],
                                           lambda e, reg=reg, c0=c0, hp=hp:
                                           e.tensor_reduce(red8[:, hp:hp + 1],
                                                           reg[:, c0:c0 + 512],
                                                           AX, ALU.max))
                    cred_toks.append(creg_user[hp % 4])
                v_cred = st("vector", cred_toks[-1:],
                            lambda e: e.tensor_max(mk2[:], mk2[:], red8[:]),
                            drain=True)

                vtoks = proj(wvs, bv_f, vst, [v_wqkv, d_vtspill] + ktoks[-2:])
                ttr_prev = v_evtr[t]
                ttoks = []
                for t4v in range(4):
                    def f(e, t4v=t4v):
                        for j in range(NCH):
                            r = e.transpose(PSCB[:, 128 * j:128 * (j + 1)],
                                            vst[:, j, 128 * t4v:128 * (t4v + 1)], identb[:])
                        return r
                    t_vt = st("tensor", vtoks + [ttr_prev], f)

                    def f(e, t4v=t4v):
                        src = PSCB[:, :].rearrange("p (j h s) -> p j h s", j=8, h=2)
                        e.tensor_copy(vtst[:, t4v, :, 0:64], src[:, :, 0, :])
                        return e.tensor_copy(vtst[:, t4v, :, 65:129], src[:, :, 1, :])
                    ttr_prev = st("vector", [t_vt], f)
                    ttoks.append(ttr_prev)
                def f(e, grp=grp):
                    return [e.dma_start(out=vt_scr[hp, :, 4 * grp:4 * grp + 4, :],
                                        in_=vtst[:, :, hp, :]) for hp in range(NHP)]
                d_vtspill = st("gpsimd", ttoks, f, n=NHP)
                t_tr_prev = None
                t_tr_prev = st("tensor", [ttoks[-1]], lambda e: e.nop())

                if own:
                    qtoks = proj(wqs, bq_f, qst, [v_wqkv, d_qspill] + vtoks[-2:])
                    d_qspill = st("gpsimd", qtoks,
                                  lambda e, g0=g0: e.dma_start(
                                      out=q_scr[:, :, g0:g0 + 512], in_=qst[:]))
                    s_sqF = st("scalar", [qtoks[-1], v_cred, tcm_sq_user],
                               lambda e: e.activation(sqb[:, :], qst[:, :, :],
                                                      AF.Square),
                               drain=True)
                    cred_toks = []
                    for hp in range(NHP):
                        reg = (S_A, S_A, S_B, S_B)[hp % 4]
                        c0 = 512 * ((hp % 4) % 2)

                        def f(e, reg=reg, c0=c0, hp=hp):
                            return e.matmul(reg[:, c0:c0 + 512], bd_ones[:, :],
                                            sqb[:, 512 * hp:512 * (hp + 1)],
                                            start=True, stop=True)
                        t_cm = st("tensor", [s_sqF, creg_user[hp % 4]], f)
                        tcm_sq_user = t_cm
                        creg_user[hp % 4] = st("vector", [t_cm],
                                               lambda e, reg=reg, c0=c0, hp=hp:
                                               e.tensor_reduce(red8[:, hp:hp + 1],
                                                               reg[:, c0:c0 + 512],
                                                               AX, ALU.max))
                        cred_toks.append(creg_user[hp % 4])
                    v_cred = st("vector", cred_toks[-1:],
                                lambda e: e.tensor_max(mq2[:], mq2[:], red8[:]),
                                drain=True)
                t_xfm_user[gbuf] = last_mm

            def f(e):
                for j in range(NCH):
                    for i in range(NCH):
                        r = e.matmul(PSC[:, j:j + 1], wqs[:, i, 128 * j:128 * (j + 1)],
                                     xfm[:, 1, i, 511:512],
                                     start=(i == 0), stop=(i == NCH - 1))
                return r
            t_qh = st("tensor", [v_cred, v_evtr[NT - 1], v_wqkv], f)

            def f(e):
                for j in range(NCH):
                    r = e.activation(qh[:, j:j + 1], PSC[:, j:j + 1], AF.Identity,
                                     bias=bq_f[:, j:j + 1], scale=1.0)
                return r
            s_qh = st("scalar", [t_qh], f)

            s_sqh = st("scalar", [s_qh, v_cred],
                       lambda e: e.activation(sqb[:, 0:8], qh[:, :], AF.Square),
                       drain=True)
            t_cqh = st("tensor", [s_sqh, s_qh, v_cred],
                       lambda e: e.matmul(PSC[:, 0:8], bd_ones[:, :], sqb[:, 0:8],
                                          start=True, stop=True))

            def f(e):
                return e.tensor_max(mq2[:, :], mq2[:, :], PSC[:, 0:8])
            v_cred = st("vector", [t_cqh, v_cred], f, drain=True)

            v_cp = st("vector", [v_cred], lambda e: e.tensor_mul(negcp[:], mq2[:], mk2[:]),
                      drain=True)
            s_cp = st("scalar", [v_cp],
                      lambda e: e.activation(negcp[:], negcp[:], AF.Sqrt))
            v_cp2 = st("vector", [s_cp],
                       lambda e: e.tensor_scalar_mul(negcp[:], negcp[:], -ASC),
                       drain=True)
            d_cpB = st("gpsimd", [v_cp2],
                       lambda e: e.dma_start(out=negcpB[:, :], in_=negcp[64:65, :]))

            def f(e):
                for h in range(16):
                    hp = h // 2
                    src = negcp[0:1, hp:hp + 1] if h % 2 == 0 else negcpB[0:1, hp:hp + 1]
                    r = e.matmul(PSC[:, h:h + 1], ones128[0:1, :], src,
                                 start=True, stop=True)
                return r
            t_cbc = st("tensor", [v_cp2, d_cpB, v_cred], f)
            v_negc = st("vector", [t_cbc],
                        lambda e: e.tensor_copy(negC[:, :], PSC[:, 0:16]),
                        drain=True)

            if probe:
                st("gpsimd", [v_negc],
                   lambda e: e.dma_start(out=fm_dump[:, 48:64], in_=negC[:, :]))

                def f(e):
                    return [e.dma_start(out=rows_dump[3:4, 0:8], in_=mq2[0:1, :]),
                            e.dma_start(out=rows_dump[4:5, 0:8], in_=mq2[64:65, :]),
                            e.dma_start(out=rows_dump[5:6, 0:8], in_=mk2[0:1, :]),
                            e.dma_start(out=rows_dump[6:7, 0:8], in_=mk2[64:65, :])]
                st("gpsimd", [v_negc], f, n=4)

            pa_dG = pr.qtok("gpsimd")
            pa_v = (pr.sems["vs"], pr.cnt["vs"])
            pr.emit(blk)

        with ExitStack() as seg:
            asb = lambda n, s, d: seg.enter_context(nc.sbuf_tensor(n, s, d))
            kwA = asb("kwA", [128, 2, SEQ], BF16)
            kwB = asb("kwB", [128, 2, SEQ], BF16)
            vtwin = asb("vtwin", [128, 2, NKC, 130], BF16)
            qwin = asb("qwin", [128, NCH, QW], BF16)
            wwin = asb("wwin", [128, 2, NCH, D], BF16)
            pa = asb("pa", [128, 4, 1024], BF16)
            ofm = asb("ofm", [128, NCH, QW], BF16)
            obst2 = asb("obst2", [64, 2, QW], BF16)
            zrow = asb("zrow", [65, 2, QW], F32)
            rzrow = asb("rzrow", [65, 2, 2, QW], F32)
            xh_ld = rzrow[0:1, 0, :, :].rearrange("p a b -> p (a b)")
            zsp = asb("zsp", [128, 64], F32)
            rsp = asb("rsp", [128, 64], F32)
            xt2 = asb("xt2", [128, 2, D], F32)
            x12 = asb("x12", [128, 4, D], F32)
            xhf2 = asb("xhf2", [128, D], F32)
            xh2b = asb("xh2b", [128, D], BF16)
            x2fm = asb("x2fm", [128, NCH, 513], BF16)
            bcol = asb("bcol", [128, 8], BF16)
            dmix = asb("dmix", [128, NCH, QW], BF16)
            xmix = asb("xmix", [128, NCH, QW], BF16)
            tbuf = asb("tbuf", [128, NCH, QW], BF16)
            krb = asb("krb", [128, 2, QW], BF16)
            kab = asb("kab", [128, NCH, QW], BF16)
            xh2hr = kab[0:1, 0:2, :].rearrange("p a b -> p (a b)")
            ffb = dmix
            x1h = zrow[0:1, :, :].rearrange("p a b -> p (a b)")
            x2o = xt2
            phb = asb("phb", [128, 64], BF16)
            oh_un = asb("oh_un", [65, 2, 8], BF16)
            zh = asb("zh", [65, 16], F32)
            rh = asb("rh", [65, 16], F32)
            rbh = asb("rbh", [128, 8], BF16)
            ohfm = asb("ohfm", [128, 8], BF16)

            bh12 = asb("bh12", [1, 12], F32)
            mvh = asb("mvh", [1, 2], F32)
            qvh = asb("qvh", [1, 1], F32)
            rsh = asb("rsh", [1, 1], F32)
            bcolH = asb("bcolH", [128, 8], BF16)
            mrh = asb("mrh", [1, 2], F32)
            tmp8 = asb("tmp8", [128, 8], BF16)
            bn12b = asb("bn12b", [128, 12], F32)
            mv2b = asb("mv2b", [128, 2], F32)
            m2c = asb("m2c", [128, 4], F32)
            w2c = asb("w2c", [128, 4], F32)
            r2c = asb("r2c", [128, 4], F32)

            t_kv_user = [None, None]
            d_obs = [None, None]
            s_ph_prev = None
            v_oevh_prev = None
            s_qh_tok = (pr.sems["ss"], pr.cnt["ss"])
            t_qwin_user = None
            v_norm_prev = None
            d_y_all = []
            d_y_prev = [None, None]
            pv_user = [None] * 4
            eidx = 0
            sexpA_prev = sexpB_prev = None
            d_zh = None

            def f(e):
                e.memset(kwA[64:128, :, :], 0.0)
                return e.memset(kwB[0:64, :, :], 0.0)
            v_kz = st("vector", [], f, drain=True)
            t_kv_user = [v_kz, v_kz]
            d_wp0 = st("gpsimd", [],
                       lambda e: e.dma_start(out=wwin[:, 0, :, :],
                                             in_=w_scr["wp"][:, :, :]))
            d_wrt0 = st("gpsimd", [],
                        lambda e: e.dma_start(out=wwin[:, 1, :, :],
                                              in_=w_scr["wrt"][:, :, :]))

            def mk_downstream(qg, d_r1_tok, d_obs_snap, d_wp_tok, d_wrt_tok,
                              v_oevh_tok, s_ph_tok, X):

                S = {"v_norm": None, "v_x1h": None}

                def res(t):
                    return t
                M = []

                if qg == 0:
                    def m(e=None):
                        S["v_zrh"] = st("vector", [v_oevh_tok],
                                        lambda e: e.reciprocal(rh[64:65, 0:16],
                                                               zh[64:65, 0:16]),
                                        drain=True)

                        def f(e):
                            e.matmul(PSC[0:64, 0:8], ones64p[64:65, 0:64],
                                     rh[64:65, 0:16:2], start=True, stop=True)
                            return e.matmul(PSC[64:128, 0:8], ones64p[64:65, 0:64],
                                            rh[64:65, 1:16:2], start=True, stop=True,
                                            tile_position=(64, 64))
                        S["t_rbh"] = st("tensor", [S["v_zrh"], s_ph_tok], f)
                    M.append(m)

                    def m():
                        v_rbh = st("vector", [S["t_rbh"]],
                                   lambda e: e.tensor_copy(rbh[:, :], PSC[:, 0:8]))
                        S["v_rbh"] = v_rbh
                        v_oha = st("vector", [v_rbh],
                                   lambda e: e.tensor_copy(ohfm[0:64, :],
                                                           oh_un[0:64, 0, :]))
                        d_ohs = st("gpsimd", [v_oha],
                                   lambda e: e.dma_start(out=ohfm[64:128, :],
                                                         in_=oh_un[0:64, 1, :]))
                        S["v_nh"] = st("vector", [d_ohs],
                                       lambda e: e.tensor_mul(ohfm[:, :], ohfm[:, :],
                                                              rbh[:, :]),
                                       drain=True)
                    M.append(m)

                    def m():
                        d_xh = st("sync", [],
                                  lambda e: e.dma_start(out=xh_ld,
                                                        in_=x_t[NT - 1, 127:128, :]))

                        def f(e):
                            return [e.dma_start(out=mrh[0:1, 0:1],
                                                in_=mfc[127:128, 31:32]),
                                    e.dma_start(out=mrh[0:1, 1:2],
                                                in_=rfc[127:128, 31:32])]
                        d_mrh = st("gpsimd", [pa_v], f, n=2)

                        def f(e):
                            e.tensor_scalar(x1h, xh_ld, mrh[0:1, 0:1], None,
                                            ALU.subtract)
                            e.drain()
                            e.tensor_scalar_mul(x1h, x1h, mrh[0:1, 1:2])
                            e.drain()
                            e.tensor_mul(x1h, x1h, ga1t[0:1, :])
                            e.drain()
                            e.tensor_add(x1h, x1h, ga0t[0:1, :])
                            e.drain()
                            return e.tensor_add(x1h, x1h, xh_ld)
                        S["v_ah"] = st("vector", [d_xh, d_mrh], f, drain=True)
                    M.append(m)

                    def mhow(n):
                        def m():
                            def f(e, n=n):
                                for vc in range(NCH):
                                    r = e.matmul(PSC[0:1, :], ohfm[:, vc:vc + 1],
                                                 wwin[:, 0, vc, 512 * n:512 * (n + 1)],
                                                 start=(vc == 0), stop=(vc == NCH - 1))
                                return r
                            t_owph = st("tensor", [S["v_nh"], res(d_wp_tok),
                                                   S["v_rbh"], S["v_ah"]], f)
                            S["v_ah"] = st("vector", [t_owph],
                                           lambda e, n=n: e.tensor_add(
                                               x1h[:, 512 * n:512 * (n + 1)],
                                               x1h[:, 512 * n:512 * (n + 1)],
                                               PSC[0:1, :]),
                                           drain=True)
                        return m
                    M.append(mhow(0))
                    M.append(mhow(1))

                    def m():
                        def f(e):
                            e.bn_stats(bh12[:, 0:6], x1h[:, 0:512])
                            e.bn_stats(bh12[:, 6:12], x1h[:, 512:1024])
                            e.drain()
                            e.bn_aggr(mvh[:], bh12[:].rearrange("p (g s) -> p g s",
                                                                g=2))
                            e.drain()
                            e.tensor_scalar_add(qvh[:], mvh[:, 1:2], EPS)
                            e.drain()
                            return e.reciprocal(qvh[:], qvh[:])
                        S["v_x1h"] = st("vector", [S["v_ah"]], f, drain=True)
                    M.append(m)

                def mnorm(hp):
                    def m():
                        d_rz = st("gpsimd", [d_r1_tok, S.get("t_r1u%d" % (hp % 2))],
                                  lambda e, hp=hp: e.dma_start(
                                      out=rzrow[64:65, hp % 2, :, :].rearrange(
                                          "o a b -> o (a b)"),
                                      in_=rr_d[0:1, 1024 * hp:1024 * (hp + 1)]))

                        def f(e, hp=hp):
                            e.matmul(PSC[0:64, :], ones64p[64:65, 0:64],
                                     rzrow[64:65, hp % 2, 0, :], start=True, stop=True)
                            return e.matmul(PSC[64:128, :], ones64p[64:65, 0:64],
                                            rzrow[64:65, hp % 2, 1, :], start=True,
                                            stop=True, tile_position=(64, 64))
                        t_r1 = st("tensor", [d_rz, d_obs_snap[hp % 2], S["v_norm"],
                                             S["v_x1h"] if hp == 0 else None,
                                             X["psc_user"] if hp == 0 else None], f)
                        S["t_r1u%d" % (hp % 2)] = t_r1

                        def f(e, hp=hp):
                            e.tensor_mul(ofm[0:64, hp, :], ofm[0:64, hp, :],
                                         PSC[0:64, :])
                            return e.tensor_mul(ofm[64:128, hp, :],
                                                ofm[64:128, hp, :], PSC[64:128, :])
                        S["v_norm"] = st("vector", [t_r1], f)
                    return m
                for hp in range(NHP):
                    M.append(mnorm(hp))

                if probe:
                    def m():
                        X["d_dump"] = st("gpsimd", [S["v_norm"]],
                                         lambda e, qg=qg: e.dma_start(
                                             out=ofm_dump[qg, :, :, :], in_=ofm[:]))
                    M.append(m)

                def mow(t4, n):
                    def m():
                        def f(e, t4=t4, n=n):
                            for vc in range(NCH):
                                r = e.matmul(PSC[:, :],
                                             ofm[:, vc, 128 * t4:128 * (t4 + 1)],
                                             wwin[:, 0, vc, 512 * n:512 * (n + 1)],
                                             start=(vc == 0), stop=(vc == NCH - 1))
                            return r
                        t_ow = st("tensor", [S["v_norm"], res(d_wp_tok),
                                             S.get("ocp", S["v_x1h"])], f)
                        S["ocp"] = st("vector", [t_ow],
                                      lambda e, t4=t4, n=n: e.tensor_copy(
                                          x12[:, t4, 512 * n:512 * (n + 1)],
                                          PSC[:, :]))
                        S["t_ow_last"] = t_ow
                        if t4 == 3:
                            S["ocp%d" % t4] = S["ocp"]
                            X["t_owp3"] = t_ow
                        else:
                            S["ocp%d" % t4] = S["ocp"]
                    return m
                for t4 in range(4):
                    for n in range(2):
                        M.append(mow(t4, n))

                def m():
                    S["d_wkt"] = st("gpsimd", [S["t_ow_last"]],
                                    lambda e: e.dma_start(out=wwin[:, 0, :, :],
                                                          in_=w_scr["wkt"][:, :, :]))
                M.append(m)

                def mxr(t4):
                    def m():
                        tc = 4 * qg + t4
                        d_x2 = st("sync", [S.get("v_xr%d" % (t4 - 2)),
                                           d_y_prev[t4 % 2]],
                                  lambda e, tc=tc, t4=t4: e.dma_start(
                                      out=xt2[:, t4 % 2, :], in_=x_t[tc, :, :]))

                        def f(e, t4=t4, tc=tc):
                            e.tensor_scalar(xhf2[:], xt2[:, t4 % 2, :],
                                            mfc[:, tc:tc + 1], None, ALU.subtract)
                            e.drain()
                            e.tensor_scalar_mul(xhf2[:], xhf2[:], rfc[:, tc:tc + 1])
                            e.drain()
                            e.tensor_mul(xhf2[:], xhf2[:], ga1t[:])
                            e.drain()
                            e.tensor_add(xhf2[:], xhf2[:], ga0t[:])
                            e.drain()
                            e.tensor_add(xhf2[:], xhf2[:], xt2[:, t4 % 2, :])
                            e.drain()
                            return e.tensor_add(x12[:, t4, :], x12[:, t4, :], xhf2[:])
                        v_xr = st("vector", [d_x2, S["ocp%d" % t4]], f, drain=True)
                        S["v_xr%d" % t4] = v_xr

                        def f(e, t4=t4):
                            e.bn_stats(bn12b[:, 0:6], x12[:, t4, 0:512])
                            e.bn_stats(bn12b[:, 6:12], x12[:, t4, 512:1024])
                            e.drain()
                            e.bn_aggr(mv2b[:], bn12b[:].rearrange("p (g s) -> p g s",
                                                                  g=2))
                            e.drain()
                            e.tensor_copy(m2c[:, t4:t4 + 1], mv2b[:, 0:1])
                            e.tensor_scalar_add(w2c[:, t4:t4 + 1], mv2b[:, 1:2], EPS)
                            e.drain()
                            return e.reciprocal(r2c[:, t4:t4 + 1], w2c[:, t4:t4 + 1])
                        S["v_bn2"] = st("vector", [v_xr], f, drain=True)
                    return m
                for t4 in range(4):
                    M.append(mxr(t4))

                def m():
                    def f(e):
                        r = e.activation(r2c[:, :], r2c[:, :], AF.Sqrt)
                        if qg == 0:
                            r = e.activation(rsh[:, :], qvh[:, :], AF.Sqrt)
                        return r
                    S["s_r2"] = st("scalar", [S["v_bn2"], S["v_x1h"]], f, drain=True)
                M.append(m)

                if qg == 0:
                    def m():
                        def f(e):
                            e.tensor_scalar(xhf2[0:1, :], x1h, mvh[:, 0:1], None,
                                            ALU.subtract)
                            e.drain()
                            return e.tensor_scalar_mul(xh2hr, xhf2[0:1, :],
                                                       rsh[:, 0:1])
                        v_xh2h = st("vector", [S["s_r2"]], f, drain=True)

                        def f(e):
                            for j in range(NCH):
                                r = e.matmul(PSC[:, j:j + 1],
                                             xh2hr[0:1, 128 * j:128 * (j + 1)],
                                             one11b[0:1, 0:1], start=True, stop=True)
                            return r
                        t_bch = st("tensor", [v_xh2h, S["ocp"]], f)
                        v_bch = st("vector", [t_bch],
                                   lambda e: e.tensor_copy(bcolH[:, :], PSC[:, 0:8]))
                        S["psc_free"] = v_bch

                        def f(e):
                            e.tensor_scalar_mul(tmp8[:], bcolH[:], psv[:, 0:1])
                            e.drain()
                            return e.tensor_add(
                                x2fm[:, :, 0:1].rearrange("p i o -> p (i o)"),
                                tmp8[:], zterm[:, :])
                        S["v_bc"] = st("vector", [v_bch, v_zterm], f, drain=True)
                    M.append(m)
                else:
                    def m():
                        S["v_bc"] = st("vector", [X["v_x2fm"]],
                                       lambda e: e.tensor_copy(
                                           x2fm[:, :, 0:1].rearrange(
                                               "p i o -> p (i o)"),
                                           bcol[:, :]), drain=True)
                        S["psc_free"] = S["ocp"]
                    M.append(m)

                def mtr(t4):
                    def m():
                        def f(e, t4=t4):
                            e.tensor_scalar(xhf2[:], x12[:, t4, :], m2c[:, t4:t4 + 1],
                                            None, ALU.subtract)
                            e.drain()
                            return e.tensor_scalar_mul(xh2b[:], xhf2[:],
                                                       r2c[:, t4:t4 + 1])
                        v_xh2 = st("vector", [S["s_r2"], S["v_bc"],
                                              S.get("v_tr2")], f, drain=True)

                        def f(e):
                            for j in range(NCH):
                                r = e.transpose(PSCB[:, 128 * j:128 * (j + 1)],
                                                xh2b[:, 128 * j:128 * (j + 1)],
                                                identb[:])
                            return r
                        t_tr2 = st("tensor", [v_xh2, S.get("t_tr2"),
                                              X["v_x2_last"] if t4 == 0 else None], f)
                        S["t_tr2"] = t_tr2
                        S["v_tr2"] = st("vector", [t_tr2],
                                        lambda e, t4=t4: e.tensor_copy(
                                            x2fm[:, :,
                                                 1 + 128 * t4:1 + 128 * (t4 + 1)],
                                            PSCB[:, :].rearrange("p (i s) -> p i s",
                                                                i=8)))
                    return m
                for t4 in range(4):
                    M.append(mtr(t4))

                def m():
                    X["v_x2fm"] = st("vector", [S["v_tr2"]],
                                     lambda e: e.tensor_copy(
                                         bcol[:, :],
                                         x2fm[:, :, 512:513].rearrange(
                                             "p i o -> p (i o)")),
                                     drain=True)

                    def f(e):
                        e.tensor_sub(dmix[:, :, :], x2fm[:, :, 1:513],
                                     x2fm[:, :, 0:512])
                        e.drain()
                        for i in range(NCH):
                            r = e.scalar_tensor_tensor(xmix[:, i, :], dmix[:, i, :],
                                                       muR[:, i:i + 1],
                                                       x2fm[:, i, 0:512],
                                                       ALU.mult, ALU.add)
                        return r
                    S["v_mix"] = st("vector", [S["v_tr2"], X["v_x2fm"], d_small], f,
                                    drain=True)
                M.append(m)

                def mr(j):
                    def m():
                        def f(e, j=j):
                            for i in range(NCH):
                                r = e.matmul(PSC[:, :],
                                             wwin[:, 1, i, 128 * j:128 * (j + 1)],
                                             xmix[:, i, :], start=(i == 0),
                                             stop=(i == NCH - 1))
                            return r
                        t_rmm = st("tensor", [S["v_mix"], res(d_wrt_tok),
                                              S["psc_free"]], f)
                        S["t_rmm_last"] = t_rmm
                        S["psc_free"] = st("scalar", [t_rmm],
                                           lambda e, j=j: e.activation(
                                               tbuf[:, j, :], PSC[:, :], AF.Tanh,
                                               bias=brh_f[:, j:j + 1], scale=0.5),
                                           drain=(j == 0))
                    return m
                for j in range(NCH):
                    M.append(mr(j))

                def m():
                    def f(e):
                        for i in range(NCH):
                            r = e.scalar_tensor_tensor(xmix[:, i, :], dmix[:, i, :],
                                                       muK[:, i:i + 1],
                                                       x2fm[:, i, 0:512],
                                                       ALU.mult, ALU.add)
                        return r
                    S["v_mixk"] = st("vector", [S["t_rmm_last"]], f, drain=True)
                M.append(m)

                def mk(j):
                    def m():
                        def f(e, j=j):
                            for i in range(NCH):
                                r = e.matmul(PSC[:, :],
                                             wwin[:, 0, i, 128 * j:128 * (j + 1)],
                                             xmix[:, i, :], start=(i == 0),
                                             stop=(i == NCH - 1))
                            return r
                        t_kmm = st("tensor", [S["v_mixk"], S["d_wkt"],
                                              S["psc_free"]], f)
                        S["t_kmm_last"] = t_kmm

                        def f(e, j=j):
                            e.tensor_scalar(krb[:, j % 2, :], PSC[:, :],
                                            bk2_f[:, j:j + 1], None, ALU.add)
                            e.drain()
                            return e.scalar_tensor_tensor(kab[:, j, :],
                                                          krb[:, j % 2, :], 0.0,
                                                          krb[:, j % 2, :],
                                                          ALU.max, ALU.mult)
                        S["psc_free"] = st("vector", [t_kmm], f)
                        X["t_w0"] = t_kmm
                    return m
                for j in range(NCH):
                    M.append(mk(j))

                def m():
                    X["d_wp_next"] = st("gpsimd", [S["t_kmm_last"]],
                                        lambda e: e.dma_start(
                                            out=wwin[:, 0, :, :],
                                            in_=w_scr["wp"][:, :, :]))
                M.append(m)

                def m():
                    S["d_wvt"] = st("gpsimd", [S["t_rmm_last"]],
                                    lambda e: e.dma_start(out=wwin[:, 1, :, :],
                                                          in_=w_scr["wvt"][:, :, :]))
                    S["v_krka_last"] = S["psc_free"]
                M.append(m)

                def mv(j):
                    def m():
                        def f(e, j=j):
                            for i in range(NCH):
                                r = e.matmul(PSC[:, :],
                                             wwin[:, 1, i, 128 * j:128 * (j + 1)],
                                             kab[:, i, :], start=(i == 0),
                                             stop=(i == NCH - 1))
                            return r
                        t_vmm = st("tensor", [S["v_krka_last"], S["d_wvt"],
                                              S["psc_free"]], f)
                        S["psc_free"] = st("vector", [t_vmm],
                                           lambda e, j=j: e.scalar_tensor_tensor(
                                               ffb[:, j, :], tbuf[:, j, :], 1.0,
                                               PSC[:, :], ALU.add, ALU.mult),
                                           drain=(j == 0))
                        X["t_w1"] = t_vmm
                        if j == NCH - 1:
                            X["psc_user"] = S["psc_free"]
                            S["v_ff_last"] = S["psc_free"]
                    return m
                for j in range(NCH):
                    M.append(mv(j))

                def m():
                    X["d_wrt_next"] = st("gpsimd", [X["t_w1"]],
                                         lambda e: e.dma_start(
                                             out=wwin[:, 1, :, :],
                                             in_=w_scr["wrt"][:, :, :]))
                M.append(m)

                def mft(t4):
                    def m():
                        def f(e, t4=t4):
                            for j in range(NCH):
                                r = e.transpose(PSCB[:, 128 * j:128 * (j + 1)],
                                                ffb[:, j, 128 * t4:128 * (t4 + 1)],
                                                identb[:])
                            return r
                        t_ftr = st("tensor", [S["v_ff_last"], S.get("t_ftr"),
                                              S.get("v_x2")], f)
                        S["t_ftr"] = t_ftr

                        def f(e, t4=t4):
                            return e.tensor_add(x2o[:, t4 % 2, :], x12[:, t4, :],
                                                PSCB[:, :])
                        v_x2 = st("vector", [t_ftr, d_y_prev[t4 % 2]], f, drain=True)
                        S["v_x2"] = v_x2
                        d_y = st("sync", [v_x2],
                                 lambda e, qg=qg, t4=t4: e.dma_start(
                                     out=y_t[4 * qg + t4, :, :],
                                     in_=x2o[:, t4 % 2, :]))
                        d_y_prev[t4 % 2] = d_y
                        d_y_all.append(d_y)
                        if t4 == 3:
                            X["v_x2_last"] = v_x2
                    return m
                for t4 in range(4):
                    M.append(mft(t4))
                return M



            pending = []
            X = {"t_owp3": None, "psc_user": None, "v_x2_last": None,
                 "v_oev_last": None, "t_w0": None, "t_w1": None, "v_x2fm": None}

            for qg in range(NQG):
                q0 = QW * qg
                d_q = st("sync", pa_dG + [t_qwin_user],
                         lambda e, q0=q0: e.dma_start(out=qwin[:],
                                                      in_=q_scr[:, :, q0:q0 + QW]))

                v_oev = [None] * NHP
                t_pv_done = [None] * NHP
                v_oev_last_prev = X["v_oev_last"]
                for hp in range(NHP):
                    kb = hp % 2

                    def f(e, hp=hp, kb=kb):
                        return [e.dma_start(out=kwA[0:64, kb, :], in_=k_scr[0:64, hp, :]),
                                e.dma_start(out=kwB[64:128, kb, :],
                                            in_=k_scr[64:128, hp, :]),
                                e.dma_start(out=vtwin[:, kb, :, :],
                                            in_=vt_scr[hp, :, :, :])]
                    d_kv = st("sync", pa_dG + [t_kv_user[kb]], f, n=3)

                    def mk_sA(p, hp=hp, kb=kb):
                        def f(e, p=p, hp=hp, kb=kb):
                            kc0, kc1 = 2 * p, 2 * p + 1
                            e.matmul(S_A[:, 0:512],
                                     kwA[:, kb, 128 * kc0:128 * (kc0 + 1)],
                                     qwin[:, hp, :], start=True, stop=True)
                            return e.matmul(S_A[:, 512:1024],
                                            kwA[:, kb, 128 * kc1:128 * (kc1 + 1)],
                                            qwin[:, hp, :], start=True, stop=True)
                        return f

                    def mk_sB(p, hp=hp, kb=kb):
                        def f(e, p=p, hp=hp, kb=kb):
                            kc0, kc1 = 2 * p, 2 * p + 1
                            e.matmul(S_B[:, 0:512],
                                     kwB[:, kb, 128 * kc0:128 * (kc0 + 1)],
                                     qwin[:, hp, :], start=True, stop=True)
                            return e.matmul(S_B[:, 512:1024],
                                            kwB[:, kb, 128 * kc1:128 * (kc1 + 1)],
                                            qwin[:, hp, :], start=True, stop=True)
                        return f

                    tsA = st("tensor", [d_kv, d_q, sexpA_prev,
                                        v_negc if (qg == 0 and hp == 0) else None],
                             mk_sA(0))
                    tsB = st("tensor", [d_kv, sexpB_prev], mk_sB(0))

                    t_pvA = t_pvB = None
                    for p in range(16):
                        kc0, kc1 = 2 * p, 2 * p + 1
                        sA = eidx % 4
                        sexpA = st("scalar", [tsA, pv_user[sA]],
                                   lambda e, hp=hp, sA=sA: e.activation(
                                       pa[:, sA, :], S_A[:, :], AF.Exp,
                                       bias=negC[:, 2 * hp:2 * hp + 1], scale=ASC),
                                   drain=(eidx % 4 == 0))
                        eidx += 1
                        if p < 15:
                            tsA = st("tensor", [sexpA], mk_sA(p + 1))

                        def f(e, kb=kb, kc0=kc0, kc1=kc1, sA=sA):
                            e.matmul(PO_A[0:65, :], vtwin[:, kb, kc0, 0:65],
                                     pa[:, sA, 0:512], start=(kc0 == 0), stop=False)
                            return e.matmul(PO_A[0:65, :], vtwin[:, kb, kc1, 0:65],
                                            pa[:, sA, 512:1024],
                                            start=False, stop=(kc1 == NKC - 1))
                        t_pvA = st("tensor", [sexpA,
                                              v_oev[hp - 1] if (p == 0 and hp > 0)
                                              else None,
                                              v_oevh_prev if p == 0 else None,
                                              v_oev_last_prev if (p == 0 and hp == 0)
                                              else None], f)
                        pv_user[sA] = t_pvA

                        sB = eidx % 4
                        sexpB = st("scalar", [tsB, pv_user[sB]],
                                   lambda e, hp=hp, sB=sB: e.activation(
                                       pa[:, sB, :], S_B[:, :], AF.Exp,
                                       bias=negC[:, 2 * hp + 1:2 * hp + 2], scale=ASC))
                        eidx += 1
                        if p < 15:
                            tsB = st("tensor", [sexpB], mk_sB(p + 1))

                        def f(e, kb=kb, kc0=kc0, kc1=kc1, sB=sB):
                            e.matmul(PO_B[0:65, :], vtwin[:, kb, kc0, 65:130],
                                     pa[:, sB, 0:512], start=(kc0 == 0), stop=False)
                            return e.matmul(PO_B[0:65, :], vtwin[:, kb, kc1, 65:130],
                                            pa[:, sB, 512:1024],
                                            start=False, stop=(kc1 == NKC - 1))
                        t_pvB = st("tensor", [sexpB,
                                              v_oev[hp - 1] if (p == 0 and hp > 0)
                                              else None,
                                              v_oev_last_prev if (p == 0 and hp == 0)
                                              else None], f)
                        pv_user[sB] = t_pvB
                        sexpA_prev, sexpB_prev = sexpA, sexpB
                        for _ in range(2):
                            if pending:
                                pending.pop(0)()
                    t_pv_done[hp] = t_pvB
                    t_kv_user[kb] = t_pvB

                    def f(e, hp=hp):
                        e.tensor_copy(zrow[64:65, 0, :], PO_A[64:65, :])
                        e.tensor_copy(zrow[64:65, 1, :], PO_B[64:65, :])
                        e.tensor_copy(ofm[0:64, hp, :], PO_A[0:64, :])
                        return e.tensor_copy(obst2[:, hp % 2, :], PO_B[0:64, :])
                    v_oev[hp] = st("vector", [t_pvA, t_pvB, d_zh,
                                              d_obs[hp % 2],
                                              X.get("d_dump") if hp == 0 else None,
                                              X["t_owp3"] if hp == 0 else None], f,
                                   drain=(hp % 2 == 0))
                    d_obs[hp % 2] = st("gpsimd", [v_oev[hp]],
                                       lambda e, hp=hp: e.dma_start(
                                           out=ofm[64:128, hp, :],
                                           in_=obst2[:, hp % 2, :]))
                    d_zh = st("gpsimd", [v_oev[hp]],
                              lambda e, hp=hp: e.dma_start(
                                  out=zr_d[0:1, 1024 * hp:1024 * (hp + 1)],
                                  in_=zrow[64:65, :, :].rearrange(
                                      "o a b -> o (a b)")))

                    if qg == 0:
                        def f(e, hp=hp, kb=kb):
                            for kc in range(NKC):
                                e.matmul(PSC[:, kc:kc + 1],
                                         kwA[:, kb, 128 * kc:128 * (kc + 1)],
                                         qh[:, hp:hp + 1], start=True, stop=True)
                            for kc in range(NKC):
                                r = e.matmul(PSC[:, 32 + kc:33 + kc],
                                             kwB[:, kb, 128 * kc:128 * (kc + 1)],
                                             qh[:, hp:hp + 1], start=True, stop=True)
                            return r
                        t_sh = st("tensor", [d_kv, s_qh_tok, s_ph_prev], f)

                        def f(e, hp=hp):
                            e.activation(phb[:, 0:32], PSC[:, 0:32], AF.Exp,
                                         bias=negC[:, 2 * hp:2 * hp + 1], scale=ASC)
                            return e.activation(phb[:, 32:64], PSC[:, 32:64], AF.Exp,
                                                bias=negC[:, 2 * hp + 1:2 * hp + 2],
                                                scale=ASC)
                        s_ph_prev = st("scalar", [t_sh], f, drain=True)

                        def f(e, hp=hp, kb=kb):
                            for kc in range(NKC):
                                e.matmul(PO_A[0:65, 0:1], vtwin[:, kb, kc, 0:65],
                                         phb[:, kc:kc + 1],
                                         start=(kc == 0), stop=(kc == NKC - 1))
                            for kc in range(NKC):
                                r = e.matmul(PO_A[0:65, 1:2], vtwin[:, kb, kc, 65:130],
                                             phb[:, 32 + kc:33 + kc],
                                             start=(kc == 0), stop=(kc == NKC - 1))
                            return r
                        t_pvh = st("tensor", [s_ph_prev, v_oev[hp]], f)

                        def f(e, hp=hp):
                            e.tensor_copy(zh[64:65, 2 * hp:2 * hp + 2], PO_A[64:65, 0:2])
                            return e.tensor_copy(oh_un[:, :, hp], PO_A[0:65, 0:2])
                        v_oevh_prev = st("vector", [t_pvh], f)
                        t_kv_user[kb] = t_pvh
                while pending:
                    pending.pop(0)()
                X["v_oev_last"] = v_oev[NHP - 1]
                t_qwin_user = t_pv_done[NHP - 1]

                d_z2 = st("gpsimd", [d_zh],
                          lambda e: e.dma_start(out=zsp[:, :], in_=zr_sp[:, :]))
                v_r = st("vector", [d_z2],
                         lambda e: e.reciprocal(rsp[:, :], zsp[:, :]), drain=True)
                d_r1 = st("gpsimd", [v_r],
                          lambda e: e.dma_start(out=rr_sp[:, :], in_=rsp[:, :]))

                pending = mk_downstream(qg, d_r1, list(d_obs),
                                        d_wp0 if qg == 0 else X["d_wp_next"],
                                        d_wrt0 if qg == 0 else X["d_wrt_next"],
                                        v_oevh_prev, s_ph_prev, X)

            while pending:
                pending.pop(0)()

            st("gpsimd_c", [d_y_all[-1], d_y_all[-2]], lambda e: e.engine_nop())
            pr.emit(blk)

    return nc


_BF = ml_dtypes.bfloat16


def kernel(**inputs):
    x = np.asarray(inputs["L0_attn_x"], dtype=np.float32)
    Wq = np.ascontiguousarray(np.asarray(inputs["L0_attn_Wq"], dtype=np.float32))
    Wk = np.ascontiguousarray(np.asarray(inputs["L0_attn_Wk"], dtype=np.float32))
    Wv = np.ascontiguousarray(np.asarray(inputs["L0_attn_Wv"], dtype=np.float32))
    Wp = np.ascontiguousarray(np.asarray(inputs["L0_attn_Wp"], dtype=np.float32))
    WrT = np.ascontiguousarray(np.asarray(inputs["L0_ffn_W_r"], dtype=np.float32).T)
    WkT = np.ascontiguousarray(np.asarray(inputs["L0_ffn_W_k"], dtype=np.float32).T)
    WvT = np.ascontiguousarray(np.asarray(inputs["L0_ffn_W_v"], dtype=np.float32).T)
    Wmod = np.ascontiguousarray(np.asarray(inputs["L0_wmod"], dtype=np.float32))
    mu_r = np.asarray(inputs["L0_ffn_mu_r"], dtype=np.float32)
    mu_k = np.asarray(inputs["L0_ffn_mu_k"], dtype=np.float32)
    Lc = np.asarray(inputs["L0_c"], dtype=np.float32)

    murfm = np.ascontiguousarray(mu_r.reshape(8, 128).T)
    mukfm = np.ascontiguousarray(mu_k.reshape(8, 128).T)
    ident = np.eye(128, dtype=_BF)
    identf = np.eye(128, dtype=np.float32)

    nc = build_nc()
    in_maps = []
    for c in range(8):
        b, half = c // 2, c % 2
        t0 = half * TOWN
        xr = np.ascontiguousarray(np.roll(x[b], -t0, axis=0))
        in_maps.append({
            "xfull": xr, "wq": Wq, "wk": Wk, "wv": Wv, "wp": Wp,
            "wrt": WrT, "wkt": WkT, "wvt": WvT, "wmod": Wmod,
            "c_fm": np.ascontiguousarray(Lc[b].reshape(8, 128).T),
            "mu_r_fm": murfm, "mu_k_fm": mukfm,
            "ps_col": np.full((128, 1), float(half), np.float32),
            "identb": ident, "identf": identf,
        })
    res = run_bass_kernel_spmd(nc, in_maps, core_ids=list(range(8)))
    out = np.empty((4, SEQ, D), np.float32)
    for c in range(8):
        b, half = c // 2, c % 2
        out[b, half * TOWN:(half + 1) * TOWN] = np.asarray(res.results[c]["y"])
    return out


# version 09 (final)
# speedup vs baseline: 1.0475x; 1.0475x over previous
import numpy as np
import ml_dtypes
from contextlib import ExitStack

import concourse.bass as bass
import concourse.mybir as mybir
from concourse.bass_utils import run_bass_kernel_spmd

F32 = mybir.dt.float32
BF16 = mybir.dt.bfloat16
AF = mybir.ActivationFunctionType
ALU = mybir.AluOpType
AX = mybir.AxisListType.X

D = 1024
NCH = 8
SEQ = 4096
TOWN = 2048
NT = 32
NQG = 4
QW = 512
NKC = 32
NHP = 8
EPS = 1e-5
ASC = 0.125

ENGS = ("sync", "gpsimd", "tensor", "vector", "scalar")
ESEM = {"tensor": "ts", "vector": "vs", "scalar": "ss", "gpsimd_c": "gs"}
QSEMS = {"sync": ("dSa", "dSb"), "gpsimd": ("dGa", "dGb")}


class P:
    def __init__(self, nc, stack):
        self.nc = nc
        self.prog = {k: [] for k in ENGS}
        self.sems = {}
        self.cnt = {}
        self.qflip = {"sync": 0, "gpsimd": 0}
        for s in ("dSa", "dSb", "dGa", "dGb", "ts", "vs", "ss", "gs"):
            self.sems[s] = stack.enter_context(nc.semaphore(s))
            self.cnt[s] = 0

    def step(self, eng, need, fn, n=1, drain=False):
        need = list(need)
        if eng in QSEMS:
            sem = QSEMS[eng][self.qflip[eng]]
            self.qflip[eng] ^= 1
            inc = 16
            need.append((self.sems[sem], self.cnt[sem]))
        else:
            sem = ESEM[eng]
            inc = 1
        self.cnt[sem] += inc * n
        h = self.sems[sem]
        tok = (h, self.cnt[sem])
        need = tuple(t for t in need if t is not None)

        def cl(e, need=need, fn=fn, h=h, inc=inc, n=n, eng=eng, drain=drain):
            if drain:
                e.drain()
            for (hh, v) in need:
                if v > 0 and not (hh is h and eng not in QSEMS):
                    e.wait_ge(hh, v)
            r = fn(e)
            rs = r if isinstance(r, list) else [r]
            assert len(rs) == n, f"step expected {n} instrs, got {len(rs)}"
            for ri in rs:
                ri.then_inc(h, inc)

        self.prog["gpsimd" if eng == "gpsimd_c" else eng].append(cl)
        return tok

    def qtok(self, eng):
        return [(self.sems[q], self.cnt[q]) for q in QSEMS[eng]]

    def emit(self, block):
        for eng in ENGS:
            fns = self.prog[eng]
            if not fns:
                continue

            def body(e, fns=list(fns)):
                for f in fns:
                    f(e)

            getattr(block, eng)(body)
        self.prog = {k: [] for k in ENGS}


def build_nc(probe=False):
    nc = bass.Bass("TRN2", target_bir_lowering=False, debug=False)

    xfull = nc.dram_tensor("xfull", [SEQ, D], F32, kind="ExternalInput").ap()
    w_in = {n: nc.dram_tensor(n, [D, D], F32, kind="ExternalInput").ap()
            for n in ("wq", "wk", "wv", "wp", "wrt", "wkt", "wvt")}
    wmod = nc.dram_tensor("wmod", [D, 6 * D], F32, kind="ExternalInput").ap()
    cfm_d = nc.dram_tensor("c_fm", [128, 8], F32, kind="ExternalInput").ap()
    mur_d = nc.dram_tensor("mu_r_fm", [128, 8], F32, kind="ExternalInput").ap()
    muk_d = nc.dram_tensor("mu_k_fm", [128, 8], F32, kind="ExternalInput").ap()
    psv_d = nc.dram_tensor("ps_col", [128, 1], F32, kind="ExternalInput").ap()
    id_d = nc.dram_tensor("identb", [128, 128], BF16, kind="ExternalInput").ap()
    idf_d = nc.dram_tensor("identf", [128, 128], F32, kind="ExternalInput").ap()
    y = nc.dram_tensor("y", [TOWN, D], F32, kind="ExternalOutput").ap()

    scrk = "ExternalOutput" if probe else "Internal"
    k_scr = nc.dram_tensor("k_scr", [128, NCH, SEQ], BF16, kind=scrk).ap()
    vt_scr = nc.dram_tensor("vt_scr", [NHP, 128, NKC, 130], BF16, kind=scrk).ap()
    q_scr = nc.dram_tensor("q_scr", [128, NCH, TOWN], BF16, kind=scrk).ap()
    w_scr = {n: nc.dram_tensor(n + "_scr", [128, NCH, D], BF16, kind=scrk).ap()
             for n in ("wp", "wrt", "wkt", "wvt")}
    zr_d = nc.dram_tensor("zr_d", [1, 8192], F32, kind="Internal").ap()
    rr_d = nc.dram_tensor("rr_d", [1, 8192], F32, kind="Internal").ap()
    if probe:
        rows_dump = nc.dram_tensor("rows_dump", [12, D], F32, kind="ExternalOutput").ap()
        fm_dump = nc.dram_tensor("fm_dump", [128, 64], F32, kind="ExternalOutput").ap()
        ofm_dump = nc.dram_tensor("ofm_dump", [NQG, 128, NCH, QW], BF16,
                                  kind="ExternalOutput").ap()
        fm2_dump = nc.dram_tensor("fm2_dump", [128, 64], F32, kind="ExternalOutput").ap()

    wview = {n: ap.rearrange("(i p) n -> p i n", p=128) for n, ap in w_in.items()}
    wmodv = wmod.rearrange("(i p) n -> p i n", p=128)
    x_t = xfull.rearrange("(t p) d -> t p d", p=128)
    y_t = y.rearrange("(t p) d -> t p d", p=128)
    zr_sp = zr_d.rearrange("o (p f) -> (o p) f", p=128)
    rr_sp = rr_d.rearrange("o (p f) -> (o p) f", p=128)

    with ExitStack() as big:
        sbuf = lambda n, s, d: big.enter_context(nc.sbuf_tensor(n, s, d))
        psum = lambda n, s, d: big.enter_context(nc.psum_tensor(n, s, d))

        negC = sbuf("negC", [128, 16], F32)
        mfc = sbuf("mfc", [128, NT], F32)
        rfc = sbuf("rfc", [128, NT], F32)
        ga1t = sbuf("ga1t", [128, D], F32)
        ga0t = sbuf("ga0t", [128, D], F32)
        muR = sbuf("muR", [128, 8], F32)
        muK = sbuf("muK", [128, 8], F32)
        psv = sbuf("psv", [128, 1], F32)
        zterm = sbuf("zterm", [128, 8], BF16)
        bq_f = sbuf("bq_f", [128, 8], F32)
        bk_f = sbuf("bk_f", [128, 8], F32)
        bv_f = sbuf("bv_f", [128, 8], F32)
        brh_f = sbuf("brh_f", [128, 8], F32)
        bk2_f = sbuf("bk2_f", [128, 8], F32)
        identb = sbuf("identb_s", [128, 128], BF16)
        ones128 = sbuf("ones128", [1, 128], F32)
        one11 = sbuf("one11", [1, 1], F32)
        ones64p = sbuf("ones64p", [65, 64], F32)
        bd_ones = sbuf("bd_ones", [128, 128], BF16)
        mq2 = sbuf("mq2", [128, 8], F32)
        mk2 = sbuf("mk2", [128, 8], F32)
        negcp = sbuf("negcp", [128, 8], F32)
        negcpB = sbuf("negcpB", [1, 8], F32)
        qh = sbuf("qh", [128, 8], BF16)
        one11b = sbuf("one11b", [1, 1], BF16)

        S_A = psum("S_A", [128, 1024], F32)
        S_B = psum("S_B", [128, 1024], F32)
        PO_A = psum("PO_A", [128, 512], F32)
        PO_B = psum("PO_B", [128, 512], F32)
        PSC = psum("PSC", [128, 512], F32)
        PSCB = psum("PSCB", [128, 1024], BF16)

        blk = big.enter_context(nc.Block())
        pr = P(nc, big)
        st = pr.step

        with ExitStack() as seg:
            asb = lambda n, s, d: seg.enter_context(nc.sbuf_tensor(n, s, d))
            wqs = asb("wqs", [128, NCH, D], BF16)
            wks = asb("wks", [128, NCH, D], BF16)
            wvs = asb("wvs", [128, NCH, D], BF16)
            wfold = asb("wfold", [128, NCH, D], BF16)
            wmgf = asb("wmgf", [128, 2, NCH, 512], F32)
            cfm = asb("cfm", [128, 8], F32)
            tnh = asb("tnh", [128, 8], F32)
            silu_f = asb("silu_f", [128, 8], F32)
            idf = asb("idf", [128, 128], F32)
            rowbuf = asb("rowbuf", [1, D], F32)
            sa_fm = asb("sa_fmv", [128, 8], F32)
            ca_fm = asb("ca_fm", [128, 8], F32)
            sbv_fm = asb("sbv_fm", [128, 8], F32)
            cb_fm = asb("cb_fm", [128, 8], F32)
            ga_fm = asb("ga_fm", [128, 8], F32)
            gb_fm = asb("gb_fm", [128, 8], F32)
            t0_fm = asb("t0_fm", [128, 8], F32)
            t1_fm = asb("t1_fm", [128, 8], F32)
            ga1_fm = asb("ga1_fm", [128, 8], F32)
            ga0_fm = asb("ga0_fm", [128, 8], F32)
            g05_fm = asb("g05_fm", [128, 8], F32)
            sa_bf = asb("sa_bf", [128, 8], BF16)
            sb_bf = asb("sb_bf", [128, 8], BF16)
            opca_fm = asb("opca_fm", [128, 8], F32)
            opcb_fm = asb("opcb_fm", [128, 8], F32)
            zf = asb("zf", [128, 8], F32)
            om = asb("om", [128, 1], F32)
            gabc = asb("gabc", [128, D], BF16)
            gb05bc = asb("gb05bc", [128, D], BF16)
            xt = asb("xt", [128, 2, D], F32)
            bn12 = asb("bn12", [128, 12], F32)
            mv2 = asb("mv2", [128, 2], F32)
            wvar = asb("wvar", [128, 1], F32)
            qvar = asb("qvar", [128, 1], F32)
            xhf = asb("xhf", [128, D], F32)
            xhb = asb("xhb", [128, D], BF16)
            xfm = asb("xfm", [128, 2, NCH, 512], BF16)
            kst = asb("kst", [128, NCH, 512], BF16)
            vst = asb("vst", [128, NCH, 512], BF16)
            qst = asb("qst", [128, NCH, 512], BF16)
            vtst = asb("vtst", [128, 4, NHP, 130], BF16)
            sqb = asb("sqb", [128, 4096], BF16)
            red8 = asb("red8", [128, 8], F32)

            def f(e):
                e.memset(ones128[:], 1.0)
                e.memset(one11[:], 1.0)
                e.memset(one11b[:], 1.0)
                e.memset(ones64p[64:65, :], 1.0)
                e.memset(bd_ones[:], 0.0)
                e.drain()
                e.memset(bd_ones[0:64, 0:64], 1.0)
                e.memset(bd_ones[64:128, 64:128], 1.0)
                e.memset(mq2[:], 0.0)
                e.memset(mk2[:], 0.0)
                e.memset(vtst[:, :, :, 64:65], 1.0)
                return e.memset(vtst[:, :, :, 129:130], 1.0)
            v_consts = st("vector", [], f, drain=True)

            def f(e):
                return [e.dma_start(out=cfm[:], in_=cfm_d[:, :]),
                        e.dma_start(out=muR[:], in_=mur_d[:, :]),
                        e.dma_start(out=muK[:], in_=muk_d[:, :]),
                        e.dma_start(out=psv[:], in_=psv_d[:, :]),
                        e.dma_start(out=idf[:], in_=idf_d[:, :]),
                        e.dma_start(out=identb[:], in_=id_d[:, :])]
            d_small = st("sync", [], f, n=6)

            def f(e):
                return [e.dma_start(out=wqs[:], in_=wview["wq"][:, :, :]),
                        e.dma_start(out=wks[:], in_=wview["wk"][:, :, :]),
                        e.dma_start(out=wvs[:], in_=wview["wv"][:, :, :])]
            d_wqkv = st("gpsimd", [], f, n=3)

            s_tnh = st("scalar", [d_small],
                       lambda e: e.activation(tnh[:], cfm[:], AF.Tanh, scale=0.5),
                       drain=True)

            def f(e):
                e.tensor_mul(silu_f[:], cfm[:], tnh[:])
                e.drain()
                e.tensor_add(silu_f[:], silu_f[:], cfm[:])
                e.drain()
                return e.tensor_scalar_mul(silu_f[:], silu_f[:], 0.5)
            v_silu = st("vector", [s_tnh], f, drain=True)

            prev_ev = None
            tg_hist = [None, None]
            for vi, (dst_fm, c0) in enumerate(((sa_fm, 0), (ca_fm, 1024),
                                               (ga_fm, 2048), (sbv_fm, 3072),
                                               (cb_fm, 4096), (gb_fm, 5120))):
                t_g = None
                for h in range(2):
                    d_f = st("sync", [tg_hist[h]],
                             lambda e, c0=c0, h=h: e.dma_start(
                                 out=wmgf[:, h, :, :],
                                 in_=wmodv[:, :, c0 + 512 * h:c0 + 512 * (h + 1)]))

                    def f(e, h=h):
                        for jj in range(4):
                            j = 4 * h + jj
                            for i in range(NCH):
                                r = e.matmul(PSC[:, j:j + 1],
                                             wmgf[:, h, i, 128 * jj:128 * (jj + 1)],
                                             silu_f[:, i:i + 1],
                                             start=(i == 0), stop=(i == NCH - 1))
                        return r
                    t_g = st("tensor", [d_f, v_silu, prev_ev if h == 0 else None], f)
                    tg_hist[h] = t_g
                prev_ev = st("vector", [t_g],
                             lambda e, dst_fm=dst_fm: e.tensor_copy(dst_fm[:, 0:8],
                                                                    PSC[:, 0:8]),
                             drain=True)
            v_mod = prev_ev

            def f(e):
                e.activation(t0_fm[:], ga_fm[:], AF.Sign)
                return e.activation(t1_fm[:], ga_fm[:], AF.Abs)
            s_sg1 = st("scalar", [v_mod], f, drain=True)

            def f(e):
                e.tensor_scalar_max(t1_fm[:], t1_fm[:], 0.25)
                e.drain()
                return e.tensor_mul(ga_fm[:], t0_fm[:], t1_fm[:])
            v_ga = st("vector", [s_sg1], f, drain=True)

            def f(e):
                e.activation(t0_fm[:], gb_fm[:], AF.Sign)
                return e.activation(t1_fm[:], gb_fm[:], AF.Abs)
            s_sg2 = st("scalar", [v_ga], f, drain=True)

            def f(e):
                e.tensor_scalar_max(t1_fm[:], t1_fm[:], 0.25)
                e.drain()
                e.tensor_mul(gb_fm[:], t0_fm[:], t1_fm[:])
                e.drain()
                e.tensor_scalar_mul(g05_fm[:], gb_fm[:], 0.5)
                e.tensor_scalar_add(opca_fm[:], ca_fm[:], 1.0)
                e.tensor_scalar_add(opcb_fm[:], cb_fm[:], 1.0)
                e.drain()
                e.tensor_mul(ga1_fm[:], ga_fm[:], opca_fm[:])
                e.tensor_mul(ga0_fm[:], ga_fm[:], sa_fm[:])
                e.tensor_copy(sa_bf[:], sa_fm[:])
                e.tensor_copy(sb_bf[:], sbv_fm[:])
                e.reciprocal(zf[:], opcb_fm[:])
                e.drain()
                e.tensor_mul(zf[:], zf[:], sbv_fm[:])
                e.drain()
                e.tensor_scalar_mul(zf[:], zf[:], -1.0)
                e.tensor_scalar(om[:], psv[:], -1.0, 1.0, ALU.mult, ALU.add)
                e.drain()
                return e.tensor_scalar_mul(zterm[:], zf[:], om[:, 0:1])
            v_zterm = st("vector", [s_sg2, d_small], f, drain=True)
            v_fmmath = v_zterm

            def bcast(v_fm, dst, need):
                def f(e, v_fm=v_fm):
                    for i in range(NCH):
                        r = e.matmul(S_A[0:1, 128 * i:128 * (i + 1)], v_fm[:, i:i + 1],
                                     idf[:, :], start=True, stop=True)
                    return r
                t_r = st("tensor", list(need) + [v_fmmath, d_small], f)
                v_row = st("vector", [t_r],
                           lambda e: e.tensor_copy(rowbuf[0:1, :], S_A[0:1, :]),
                           drain=True)

                def f(e):
                    for half in range(2):
                        r = e.matmul(S_B[:, 512 * half:512 * (half + 1)], ones128[0:1, :],
                                     rowbuf[0:1, 512 * half:512 * (half + 1)],
                                     start=True, stop=True)
                    return r
                t_b = st("tensor", [v_row, v_consts], f)
                return st("vector", [t_b],
                          lambda e, dst=dst: e.tensor_copy(dst[:], S_B[:, :]),
                          drain=True)

            v_bc1 = bcast(ga_fm, gabc, [])
            v_bc2 = bcast(g05_fm, gb05bc, [v_bc1])
            v_bc3 = bcast(ga1_fm, ga1t, [v_bc2])
            v_bc4 = bcast(ga0_fm, ga0t, [v_bc3])

            def bias_fm(wsb, vec_bf, dst, need, halve=False):
                def f(e, wsb=wsb, vec_bf=vec_bf):
                    for j in range(NCH):
                        for i in range(NCH):
                            r = e.matmul(PSC[:, j:j + 1], wsb[:, i, 128 * j:128 * (j + 1)],
                                         vec_bf[:, i:i + 1],
                                         start=(i == 0), stop=(i == NCH - 1))
                    return r
                t_b = st("tensor", list(need) + [v_fmmath], f)

                def f(e, dst=dst, halve=halve):
                    e.tensor_copy(dst[:, 0:8], PSC[:, 0:8])
                    e.drain()
                    if halve:
                        return e.tensor_scalar_mul(dst[:], dst[:], 0.5)
                    return e.tensor_scalar_mul(dst[:], dst[:], 1.0)
                return st("vector", [t_b], f, drain=True)

            vb_q = bias_fm(wqs, sa_bf, bq_f, [d_wqkv, v_bc4])
            vb_k = bias_fm(wks, sa_bf, bk_f, [vb_q])
            vb_v = bias_fm(wvs, sa_bf, bv_f, [vb_k])

            def f(e):
                for wsb in (wqs, wks, wvs):
                    for i in range(NCH):
                        r = e.tensor_scalar_mul(wsb[:, i, :], wsb[:, i, :],
                                                opca_fm[:, i:i + 1])
                return r
            v_wqkv = st("vector", [vb_v], f, drain=True)

            prev_tok = v_wqkv
            for nm in ("wp", "wrt", "wkt", "wvt"):
                d_wf = st("gpsimd", [prev_tok],
                          lambda e, nm=nm: e.dma_start(out=wfold[:],
                                                       in_=wview[nm][:, :, :]))
                vbias = None
                if nm == "wrt":
                    vbias = bias_fm(wfold, sb_bf, brh_f, [d_wf], halve=True)
                elif nm == "wkt":
                    vbias = bias_fm(wfold, sb_bf, bk2_f, [d_wf])
                fold_need = [d_wf] if vbias is None else [vbias]

                def f(e, nm=nm):
                    if nm == "wp":
                        for i in range(NCH):
                            r = e.tensor_mul(wfold[:, i, :], wfold[:, i, :], gabc[:])
                    elif nm == "wvt":
                        for i in range(NCH):
                            r = e.tensor_mul(wfold[:, i, :], wfold[:, i, :], gb05bc[:])
                    else:
                        for i in range(NCH):
                            r = e.tensor_scalar_mul(wfold[:, i, :], wfold[:, i, :],
                                                    opcb_fm[:, i:i + 1])
                    return r
                extra = [v_bc2] if nm in ("wp", "wvt") else [v_fmmath]
                v_fold = st("vector", fold_need + extra, f, drain=True)
                prev_tok = st("gpsimd", [v_fold],
                              lambda e, nm=nm: e.dma_start(out=w_scr[nm][:, :, :],
                                                           in_=wfold[:]))
            d_wspill = prev_tok

            if probe:
                def f2(e):
                    return [e.dma_start(out=fm2_dump[:, 0:8], in_=silu_f[:]),
                            e.dma_start(out=fm2_dump[:, 8:16], in_=sa_fm[:]),
                            e.dma_start(out=fm2_dump[:, 16:24], in_=ca_fm[:]),
                            e.dma_start(out=fm2_dump[:, 24:32], in_=sbv_fm[:]),
                            e.dma_start(out=fm2_dump[:, 32:40], in_=cb_fm[:]),
                            e.dma_start(out=fm2_dump[:, 40:48], in_=ga_fm[:]),
                            e.dma_start(out=fm2_dump[:, 48:56], in_=gb_fm[:])]
                st("gpsimd", [v_zterm], f2, n=7)

                def f(e):
                    return [e.dma_start(out=fm_dump[:, 0:8], in_=opca_fm[:]),
                            e.dma_start(out=fm_dump[:, 8:16], in_=opcb_fm[:]),
                            e.dma_start(out=fm_dump[:, 16:24], in_=bq_f[:]),
                            e.dma_start(out=fm_dump[:, 24:32], in_=brh_f[:]),
                            e.dma_start(out=fm_dump[:, 32:40], in_=bk2_f[:]),
                            e.dma_start(out=fm_dump[:, 40:48], in_=zf[:]),
                            e.dma_start(out=rows_dump[0:1, :], in_=rowbuf[:, :]),
                            e.dma_start(out=rows_dump[1:2, 0:8], in_=ga_fm[0:1, :]),
                            e.dma_start(out=rows_dump[2:3, :], in_=ga1t[0:1, :])]
                st("gpsimd", [v_zterm, vb_v, d_wspill, v_bc4], f, n=9)

            v_evtr = [None] * NT
            t_xfm_user = [None, None]
            creg_user = [None] * 4
            tcm_sq_user = None
            d_kspill = d_vtspill = d_qspill = None
            v_cred = None
            s_rstd_prev = None
            t_tr_prev = None

            for t in range(NT):
                buf = t % 2
                grp, t4 = t // 4, t % 4
                gbuf = grp % 2

                d_x = st("sync", [v_evtr[t - 2] if t >= 2 else None],
                         lambda e, t=t, buf=buf: e.dma_start(out=xt[:, buf, :],
                                                             in_=x_t[t, :, :]))

                def f(e, t=t, buf=buf):
                    e.bn_stats(bn12[:, 0:6], xt[:, buf, 0:512])
                    e.bn_stats(bn12[:, 6:12], xt[:, buf, 512:1024])
                    e.drain()
                    e.bn_aggr(mv2[:], bn12[:].rearrange("p (g s) -> p g s", g=2))
                    e.drain()
                    e.tensor_copy(mfc[:, t:t + 1], mv2[:, 0:1])
                    e.tensor_scalar_add(wvar[:], mv2[:, 1:2], EPS)
                    e.drain()
                    return e.reciprocal(qvar[:], wvar[:])
                v_bn = st("vector", [d_x, v_consts, s_rstd_prev], f, drain=True)

                s_rstd = st("scalar", [v_bn],
                            lambda e, t=t: e.activation(rfc[:, t:t + 1], qvar[:], AF.Sqrt))
                s_rstd_prev = s_rstd

                def f(e, t=t, buf=buf):
                    e.tensor_scalar(xhf[:], xt[:, buf, :], mfc[:, t:t + 1], None,
                                    ALU.subtract)
                    e.drain()
                    return e.tensor_scalar_mul(xhb[:], xhf[:], rfc[:, t:t + 1])
                v_xh = st("vector", [s_rstd, t_tr_prev], f, drain=True)

                def f(e):
                    for j in range(NCH):
                        r = e.transpose(PSCB[:, 128 * j:128 * (j + 1)],
                                        xhb[:, 128 * j:128 * (j + 1)], identb[:])
                    return r
                t_tr = st("tensor", [v_xh, v_evtr[t - 1] if t >= 1 else None], f)
                t_tr_prev = t_tr

                v_evtr[t] = st("vector", [t_tr, t_xfm_user[gbuf]],
                               lambda e, gbuf=gbuf, t4=t4: e.tensor_copy(
                                   xfm[:, gbuf, :, 128 * t4:128 * (t4 + 1)],
                                   PSCB[:, :].rearrange("p (i s) -> p i s", i=8)))

                if t4 != 3:
                    continue

                g0 = 512 * grp
                v_grp = v_evtr[t]
                own = grp < 4
                last_mm = None

                def proj(wsb, bias, stage, need):
                    nonlocal last_mm
                    toks = []
                    mms = []
                    for j in range(NCH):
                        po = (PO_A, PO_B)[j % 2]

                        def f(e, wsb=wsb, j=j, po=po, gbuf=gbuf):
                            for i in range(NCH):
                                r = e.matmul(po[:, :], wsb[:, i, 128 * j:128 * (j + 1)],
                                             xfm[:, gbuf, i, :],
                                             start=(i == 0), stop=(i == NCH - 1))
                            return r
                        t_mm = st("tensor", list(need) + [v_grp] +
                                  ([toks[j - 2]] if j >= 2 else []), f)
                        mms.append(t_mm)
                        toks.append(st("scalar", [t_mm],
                                       lambda e, j=j, po=po, stage=stage, bias=bias:
                                       e.activation(stage[:, j, :], po[:, :], AF.Identity,
                                                    bias=bias[:, j:j + 1], scale=1.0),
                                       drain=(j == 0)))
                    last_mm = mms[-1]
                    return toks

                ktoks = proj(wks, bk_f, kst, [v_wqkv, d_kspill, v_cred])
                d_kspill = st("gpsimd", ktoks,
                              lambda e, g0=g0: e.dma_start(
                                  out=k_scr[:, :, g0:g0 + 512], in_=kst[:]))
                s_sqF = st("scalar", [ktoks[-1], v_cred, tcm_sq_user],
                           lambda e: e.activation(sqb[:, :], kst[:, :, :], AF.Square),
                           drain=True)
                cred_toks = []
                for hp in range(NHP):
                    reg = (S_A, S_A, S_B, S_B)[hp % 4]
                    c0 = 512 * ((hp % 4) % 2)

                    def f(e, reg=reg, c0=c0, hp=hp):
                        return e.matmul(reg[:, c0:c0 + 512], bd_ones[:, :],
                                        sqb[:, 512 * hp:512 * (hp + 1)],
                                        start=True, stop=True)
                    t_cm = st("tensor", [s_sqF, creg_user[hp % 4]], f)
                    tcm_sq_user = t_cm
                    creg_user[hp % 4] = st("vector", [t_cm],
                                           lambda e, reg=reg, c0=c0, hp=hp:
                                           e.tensor_reduce(red8[:, hp:hp + 1],
                                                           reg[:, c0:c0 + 512],
                                                           AX, ALU.max))
                    cred_toks.append(creg_user[hp % 4])
                v_cred = st("vector", cred_toks[-1:],
                            lambda e: e.tensor_max(mk2[:], mk2[:], red8[:]),
                            drain=True)

                vtoks = proj(wvs, bv_f, vst, [v_wqkv, d_vtspill] + ktoks[-2:])
                ttr_prev = v_evtr[t]
                ttoks = []
                for t4v in range(4):
                    def f(e, t4v=t4v):
                        for j in range(NCH):
                            r = e.transpose(PSCB[:, 128 * j:128 * (j + 1)],
                                            vst[:, j, 128 * t4v:128 * (t4v + 1)], identb[:])
                        return r
                    t_vt = st("tensor", vtoks + [ttr_prev], f)

                    def f(e, t4v=t4v):
                        src = PSCB[:, :].rearrange("p (j h s) -> p j h s", j=8, h=2)
                        e.tensor_copy(vtst[:, t4v, :, 0:64], src[:, :, 0, :])
                        return e.tensor_copy(vtst[:, t4v, :, 65:129], src[:, :, 1, :])
                    ttr_prev = st("vector", [t_vt], f)
                    ttoks.append(ttr_prev)
                def f(e, grp=grp):
                    return [e.dma_start(out=vt_scr[hp, :, 4 * grp:4 * grp + 4, :],
                                        in_=vtst[:, :, hp, :]) for hp in range(NHP)]
                d_vtspill = st("gpsimd", ttoks, f, n=NHP)
                t_tr_prev = None
                t_tr_prev = st("tensor", [ttoks[-1]], lambda e: e.nop())

                if own:
                    qtoks = proj(wqs, bq_f, qst, [v_wqkv, d_qspill] + vtoks[-2:])
                    d_qspill = st("gpsimd", qtoks,
                                  lambda e, g0=g0: e.dma_start(
                                      out=q_scr[:, :, g0:g0 + 512], in_=qst[:]))
                    s_sqF = st("scalar", [qtoks[-1], v_cred, tcm_sq_user],
                               lambda e: e.activation(sqb[:, :], qst[:, :, :],
                                                      AF.Square),
                               drain=True)
                    cred_toks = []
                    for hp in range(NHP):
                        reg = (S_A, S_A, S_B, S_B)[hp % 4]
                        c0 = 512 * ((hp % 4) % 2)

                        def f(e, reg=reg, c0=c0, hp=hp):
                            return e.matmul(reg[:, c0:c0 + 512], bd_ones[:, :],
                                            sqb[:, 512 * hp:512 * (hp + 1)],
                                            start=True, stop=True)
                        t_cm = st("tensor", [s_sqF, creg_user[hp % 4]], f)
                        tcm_sq_user = t_cm
                        creg_user[hp % 4] = st("vector", [t_cm],
                                               lambda e, reg=reg, c0=c0, hp=hp:
                                               e.tensor_reduce(red8[:, hp:hp + 1],
                                                               reg[:, c0:c0 + 512],
                                                               AX, ALU.max))
                        cred_toks.append(creg_user[hp % 4])
                    v_cred = st("vector", cred_toks[-1:],
                                lambda e: e.tensor_max(mq2[:], mq2[:], red8[:]),
                                drain=True)
                t_xfm_user[gbuf] = last_mm

            def f(e):
                for j in range(NCH):
                    for i in range(NCH):
                        r = e.matmul(PSC[:, j:j + 1], wqs[:, i, 128 * j:128 * (j + 1)],
                                     xfm[:, 1, i, 511:512],
                                     start=(i == 0), stop=(i == NCH - 1))
                return r
            t_qh = st("tensor", [v_cred, v_evtr[NT - 1], v_wqkv], f)

            def f(e):
                for j in range(NCH):
                    r = e.activation(qh[:, j:j + 1], PSC[:, j:j + 1], AF.Identity,
                                     bias=bq_f[:, j:j + 1], scale=1.0)
                return r
            s_qh = st("scalar", [t_qh], f)

            s_sqh = st("scalar", [s_qh, v_cred],
                       lambda e: e.activation(sqb[:, 0:8], qh[:, :], AF.Square),
                       drain=True)
            t_cqh = st("tensor", [s_sqh, s_qh, v_cred],
                       lambda e: e.matmul(PSC[:, 0:8], bd_ones[:, :], sqb[:, 0:8],
                                          start=True, stop=True))

            def f(e):
                return e.tensor_max(mq2[:, :], mq2[:, :], PSC[:, 0:8])
            v_cred = st("vector", [t_cqh, v_cred], f, drain=True)

            v_cp = st("vector", [v_cred], lambda e: e.tensor_mul(negcp[:], mq2[:], mk2[:]),
                      drain=True)
            s_cp = st("scalar", [v_cp],
                      lambda e: e.activation(negcp[:], negcp[:], AF.Sqrt))
            v_cp2 = st("vector", [s_cp],
                       lambda e: e.tensor_scalar_mul(negcp[:], negcp[:], -ASC),
                       drain=True)
            d_cpB = st("gpsimd", [v_cp2],
                       lambda e: e.dma_start(out=negcpB[:, :], in_=negcp[64:65, :]))

            def f(e):
                for h in range(16):
                    hp = h // 2
                    src = negcp[0:1, hp:hp + 1] if h % 2 == 0 else negcpB[0:1, hp:hp + 1]
                    r = e.matmul(PSC[:, h:h + 1], ones128[0:1, :], src,
                                 start=True, stop=True)
                return r
            t_cbc = st("tensor", [v_cp2, d_cpB, v_cred], f)
            v_negc = st("vector", [t_cbc],
                        lambda e: e.tensor_copy(negC[:, :], PSC[:, 0:16]),
                        drain=True)

            if probe:
                st("gpsimd", [v_negc],
                   lambda e: e.dma_start(out=fm_dump[:, 48:64], in_=negC[:, :]))

                def f(e):
                    return [e.dma_start(out=rows_dump[3:4, 0:8], in_=mq2[0:1, :]),
                            e.dma_start(out=rows_dump[4:5, 0:8], in_=mq2[64:65, :]),
                            e.dma_start(out=rows_dump[5:6, 0:8], in_=mk2[0:1, :]),
                            e.dma_start(out=rows_dump[6:7, 0:8], in_=mk2[64:65, :])]
                st("gpsimd", [v_negc], f, n=4)

            pa_dG = pr.qtok("gpsimd")
            pa_v = (pr.sems["vs"], pr.cnt["vs"])
            pr.emit(blk)

        with ExitStack() as seg:
            asb = lambda n, s, d: seg.enter_context(nc.sbuf_tensor(n, s, d))
            kwA = asb("kwA", [128, 2, SEQ], BF16)
            kwB = asb("kwB", [128, 2, SEQ], BF16)
            vtwin = asb("vtwin", [128, 2, NKC, 130], BF16)
            qwin = asb("qwin", [128, NCH, QW], BF16)
            wwin = asb("wwin", [128, 2, NCH, D], BF16)
            pa = asb("pa", [128, 4, 1024], BF16)
            ofm = asb("ofm", [128, NCH, QW], BF16)
            obst2 = asb("obst2", [64, 2, QW], BF16)
            zrow = asb("zrow", [65, 2, QW], F32)
            rzrow = asb("rzrow", [65, 2, 2, QW], F32)
            xh1k = asb("xh1k", [1, D], F32)
            xh_ld = xh1k[0:1, :]
            zsp = asb("zsp", [128, 64], F32)
            rsp = asb("rsp", [128, 64], F32)
            xt2 = asb("xt2", [128, 2, D], F32)
            x12 = asb("x12", [128, 4, D], F32)
            xhf2 = asb("xhf2", [128, D], F32)
            xh2b = asb("xh2b", [128, D], BF16)
            x2fm = asb("x2fm", [128, NCH, 513], BF16)
            bcol = asb("bcol", [128, 8], BF16)
            dmix = asb("dmix", [128, NCH, QW], BF16)
            xmix = asb("xmix", [128, NCH, QW], BF16)
            tbuf = asb("tbuf", [128, NCH, QW], BF16)
            krb = asb("krb", [128, 4, QW], BF16)
            kab = qwin
            xh2hr = qwin[0:1, 0:2, :].rearrange("p a b -> p (a b)")
            ffb = dmix
            ff32 = asb("ff32", [128, D], F32)
            x1h = ff32[0:1, :]
            x1h2 = None
            x2o = xt2
            phb = asb("phb", [128, 64], BF16)
            oh_un = asb("oh_un", [65, 2, 8], BF16)
            zh = asb("zh", [65, 16], F32)
            rh = asb("rh", [65, 16], F32)
            rbh = asb("rbh", [128, 8], BF16)
            ohfm = asb("ohfm", [128, 8], BF16)

            bh12 = asb("bh12", [1, 12], F32)
            mvh = asb("mvh", [1, 2], F32)
            qvh = asb("qvh", [1, 1], F32)
            rsh = asb("rsh", [1, 1], F32)
            bcolH = asb("bcolH", [128, 8], BF16)
            mrh = asb("mrh", [1, 2], F32)
            tmp8 = asb("tmp8", [128, 8], BF16)
            bn12b = asb("bn12b", [128, 12], F32)
            mv2b = asb("mv2b", [128, 2], F32)
            m2c = asb("m2c", [128, 4], F32)
            w2c = asb("w2c", [128, 4], F32)
            r2c = asb("r2c", [128, 4], F32)

            t_kv_user = [None, None]
            t_w_user = [None, None]
            d_obs = [None, None]
            d_dump_prev = None
            s_ph_prev = None
            v_oevh_prev = None
            s_qh_tok = (pr.sems["ss"], pr.cnt["ss"])
            t_qwin_user = None
            v_x2fm_prev = None
            v_norm_prev = None
            prev_qg_last_v = pa_v
            d_y_all = []
            d_y_prev = [None, None]
            pv_user = [None] * 4
            eidx = 0
            sexpA_prev = sexpB_prev = None
            d_zh = None
            t_r1_user = [None, None]

            def f(e):
                e.memset(kwA[64:128, :, :], 0.0)
                return e.memset(kwB[0:64, :, :], 0.0)
            v_kz = st("vector", [], f, drain=True)
            t_kv_user = [v_kz, v_kz]

            for qg in range(NQG):
                q0 = QW * qg
                d_q = st("sync", pa_dG + [t_qwin_user],
                         lambda e, q0=q0: e.dma_start(out=qwin[:],
                                                      in_=q_scr[:, :, q0:q0 + QW]))
                d_wp = st("sync", pa_dG + [t_w_user[0]],
                          lambda e: e.dma_start(out=wwin[:, 0, :, :],
                                                in_=w_scr["wp"][:, :, :]))

                v_oev = [None] * NHP
                t_pv_done = [None] * NHP
                for hp in range(NHP):
                    kb = hp % 2

                    def f(e, hp=hp, kb=kb):
                        return [e.dma_start(out=kwA[0:64, kb, :], in_=k_scr[0:64, hp, :]),
                                e.dma_start(out=kwB[64:128, kb, :],
                                            in_=k_scr[64:128, hp, :]),
                                e.dma_start(out=vtwin[:, kb, :, :],
                                            in_=vt_scr[hp, :, :, :])]
                    d_kv = st("sync", pa_dG + [t_kv_user[kb]], f, n=3)

                    def mk_sA(p, hp=hp, kb=kb):
                        def f(e, p=p, hp=hp, kb=kb):
                            kc0, kc1 = 2 * p, 2 * p + 1
                            e.matmul(S_A[:, 0:512],
                                     kwA[:, kb, 128 * kc0:128 * (kc0 + 1)],
                                     qwin[:, hp, :], start=True, stop=True)
                            return e.matmul(S_A[:, 512:1024],
                                            kwA[:, kb, 128 * kc1:128 * (kc1 + 1)],
                                            qwin[:, hp, :], start=True, stop=True)
                        return f

                    def mk_sB(p, hp=hp, kb=kb):
                        def f(e, p=p, hp=hp, kb=kb):
                            kc0, kc1 = 2 * p, 2 * p + 1
                            e.matmul(S_B[:, 0:512],
                                     kwB[:, kb, 128 * kc0:128 * (kc0 + 1)],
                                     qwin[:, hp, :], start=True, stop=True)
                            return e.matmul(S_B[:, 512:1024],
                                            kwB[:, kb, 128 * kc1:128 * (kc1 + 1)],
                                            qwin[:, hp, :], start=True, stop=True)
                        return f

                    tsA = st("tensor", [d_kv, d_q, sexpA_prev,
                                        prev_qg_last_v if hp == 0 else None,
                                        v_negc if hp == 0 else None], mk_sA(0))
                    tsB = st("tensor", [d_kv, sexpB_prev], mk_sB(0))

                    t_pvA = t_pvB = None
                    for p in range(16):
                        kc0, kc1 = 2 * p, 2 * p + 1
                        sA = eidx % 4
                        sexpA = st("scalar", [tsA, pv_user[sA]],
                                   lambda e, hp=hp, sA=sA: e.activation(
                                       pa[:, sA, :], S_A[:, :], AF.Exp,
                                       bias=negC[:, 2 * hp:2 * hp + 1], scale=ASC),
                                   drain=(eidx % 4 == 0))
                        eidx += 1
                        if p < 15:
                            tsA = st("tensor", [sexpA], mk_sA(p + 1))

                        def f(e, kb=kb, kc0=kc0, kc1=kc1, sA=sA):
                            e.matmul(PO_A[0:65, :], vtwin[:, kb, kc0, 0:65],
                                     pa[:, sA, 0:512], start=(kc0 == 0), stop=False)
                            return e.matmul(PO_A[0:65, :], vtwin[:, kb, kc1, 0:65],
                                            pa[:, sA, 512:1024],
                                            start=False, stop=(kc1 == NKC - 1))
                        t_pvA = st("tensor", [sexpA,
                                              v_oev[hp - 1] if (p == 0 and hp > 0)
                                              else None,
                                              v_oevh_prev if p == 0 else None,
                                              prev_qg_last_v if (p == 0 and hp == 0)
                                              else None], f)
                        pv_user[sA] = t_pvA

                        sB = eidx % 4
                        sexpB = st("scalar", [tsB, pv_user[sB]],
                                   lambda e, hp=hp, sB=sB: e.activation(
                                       pa[:, sB, :], S_B[:, :], AF.Exp,
                                       bias=negC[:, 2 * hp + 1:2 * hp + 2], scale=ASC))
                        eidx += 1
                        if p < 15:
                            tsB = st("tensor", [sexpB], mk_sB(p + 1))

                        def f(e, kb=kb, kc0=kc0, kc1=kc1, sB=sB):
                            e.matmul(PO_B[0:65, :], vtwin[:, kb, kc0, 65:130],
                                     pa[:, sB, 0:512], start=(kc0 == 0), stop=False)
                            return e.matmul(PO_B[0:65, :], vtwin[:, kb, kc1, 65:130],
                                            pa[:, sB, 512:1024],
                                            start=False, stop=(kc1 == NKC - 1))
                        t_pvB = st("tensor", [sexpB,
                                              v_oev[hp - 1] if (p == 0 and hp > 0)
                                              else None,
                                              prev_qg_last_v if (p == 0 and hp == 0)
                                              else None], f)
                        pv_user[sB] = t_pvB
                        sexpA_prev, sexpB_prev = sexpA, sexpB
                    t_pv_done[hp] = t_pvB
                    t_kv_user[kb] = t_pvB

                    def f(e, hp=hp):
                        e.tensor_copy(zrow[64:65, 0, :], PO_A[64:65, :])
                        e.tensor_copy(zrow[64:65, 1, :], PO_B[64:65, :])
                        e.tensor_copy(ofm[0:64, hp, :], PO_A[0:64, :])
                        return e.tensor_copy(obst2[:, hp % 2, :], PO_B[0:64, :])
                    v_oev[hp] = st("vector", [t_pvA, t_pvB, d_zh,
                                              d_obs[hp % 2], d_dump_prev], f,
                                   drain=(hp % 2 == 0))
                    d_obs[hp % 2] = st("gpsimd", [v_oev[hp]],
                                       lambda e, hp=hp: e.dma_start(
                                           out=ofm[64:128, hp, :],
                                           in_=obst2[:, hp % 2, :]))
                    d_zh = st("gpsimd", [v_oev[hp]],
                              lambda e, hp=hp: e.dma_start(
                                  out=zr_d[0:1, 1024 * hp:1024 * (hp + 1)],
                                  in_=zrow[64:65, :, :].rearrange(
                                      "o a b -> o (a b)")))

                    if qg == 0:
                        def f(e, hp=hp, kb=kb):
                            for kc in range(NKC):
                                e.matmul(PSC[:, kc:kc + 1],
                                         kwA[:, kb, 128 * kc:128 * (kc + 1)],
                                         qh[:, hp:hp + 1], start=True, stop=True)
                            for kc in range(NKC):
                                r = e.matmul(PSC[:, 32 + kc:33 + kc],
                                             kwB[:, kb, 128 * kc:128 * (kc + 1)],
                                             qh[:, hp:hp + 1], start=True, stop=True)
                            return r
                        t_sh = st("tensor", [d_kv, s_qh_tok, s_ph_prev], f)

                        def f(e, hp=hp):
                            e.activation(phb[:, 0:32], PSC[:, 0:32], AF.Exp,
                                         bias=negC[:, 2 * hp:2 * hp + 1], scale=ASC)
                            return e.activation(phb[:, 32:64], PSC[:, 32:64], AF.Exp,
                                                bias=negC[:, 2 * hp + 1:2 * hp + 2],
                                                scale=ASC)
                        s_ph_prev = st("scalar", [t_sh], f, drain=True)

                        def f(e, hp=hp, kb=kb):
                            for kc in range(NKC):
                                e.matmul(PO_A[0:65, 0:1], vtwin[:, kb, kc, 0:65],
                                         phb[:, kc:kc + 1],
                                         start=(kc == 0), stop=(kc == NKC - 1))
                            for kc in range(NKC):
                                r = e.matmul(PO_A[0:65, 1:2], vtwin[:, kb, kc, 65:130],
                                             phb[:, 32 + kc:33 + kc],
                                             start=(kc == 0), stop=(kc == NKC - 1))
                            return r
                        t_pvh = st("tensor", [s_ph_prev, v_oev[hp]], f)

                        def f(e, hp=hp):
                            e.tensor_copy(zh[64:65, 2 * hp:2 * hp + 2], PO_A[64:65, 0:2])
                            return e.tensor_copy(oh_un[:, :, hp], PO_A[0:65, 0:2])
                        v_oevh_prev = st("vector", [t_pvh], f)
                        t_kv_user[kb] = t_pvh

                d_z2 = st("gpsimd", [d_zh],
                          lambda e: e.dma_start(out=zsp[:, :], in_=zr_sp[:, :]))
                v_r = st("vector", [d_z2],
                         lambda e: e.reciprocal(rsp[:, :], zsp[:, :]), drain=True)
                d_r1 = st("gpsimd", [v_r],
                          lambda e: e.dma_start(out=rr_sp[:, :], in_=rsp[:, :]))

                v_x1h = None
                if qg == 0:
                    v_zrh = st("vector", [v_oevh_prev],
                               lambda e: e.reciprocal(rh[64:65, 0:16], zh[64:65, 0:16]),
                               drain=True)

                    def f(e):
                        e.matmul(PSC[0:64, 0:8], ones64p[64:65, 0:64],
                                 rh[64:65, 0:16:2], start=True, stop=True)
                        return e.matmul(PSC[64:128, 0:8], ones64p[64:65, 0:64],
                                        rh[64:65, 1:16:2], start=True, stop=True,
                                        tile_position=(64, 64))
                    t_rbh = st("tensor", [v_zrh, s_ph_prev], f)
                    v_rbh = st("vector", [t_rbh],
                               lambda e: e.tensor_copy(rbh[:, :], PSC[:, 0:8]))
                    v_oha = st("vector", [v_rbh],
                               lambda e: e.tensor_copy(ohfm[0:64, :], oh_un[0:64, 0, :]))
                    d_ohs = st("gpsimd", [v_oha],
                               lambda e: e.dma_start(out=ohfm[64:128, :],
                                                     in_=oh_un[0:64, 1, :]))
                    v_nh = st("vector", [d_ohs],
                              lambda e: e.tensor_mul(ohfm[:, :], ohfm[:, :], rbh[:, :]),
                              drain=True)

                    def f(e):
                        for n in range(2):
                            for vc in range(NCH):
                                r = e.matmul(S_A[0:1, 512 * n:512 * (n + 1)],
                                             ohfm[:, vc:vc + 1],
                                             wwin[:, 0, vc, 512 * n:512 * (n + 1)],
                                             start=(vc == 0), stop=(vc == NCH - 1))
                        return r
                    t_owph = st("tensor", [v_nh, d_wp, prev_qg_last_v], f)
                    d_xh = st("sync", [],
                              lambda e: e.dma_start(out=xh_ld,
                                                    in_=x_t[NT - 1, 127:128, :]))

                    def f(e):
                        return [e.dma_start(out=mrh[0:1, 0:1], in_=mfc[127:128, 31:32]),
                                e.dma_start(out=mrh[0:1, 1:2], in_=rfc[127:128, 31:32])]
                    d_mrh = st("gpsimd", [pa_v], f, n=2)

                    def f(e):
                        e.tensor_scalar(x1h, xh_ld, mrh[0:1, 0:1], None, ALU.subtract)
                        e.drain()
                        e.tensor_scalar_mul(x1h, x1h, mrh[0:1, 1:2])
                        e.drain()
                        e.tensor_mul(x1h, x1h, ga1t[0:1, :])
                        e.drain()
                        e.tensor_add(x1h, x1h, ga0t[0:1, :])
                        e.drain()
                        e.tensor_add(x1h, x1h, xh_ld)
                        e.drain()
                        e.tensor_add(x1h, x1h, S_A[0:1, :])
                        e.drain()
                        e.bn_stats(bh12[:, 0:6], x1h[:, 0:512])
                        e.bn_stats(bh12[:, 6:12], x1h[:, 512:1024])
                        e.drain()
                        e.bn_aggr(mvh[:], bh12[:].rearrange("p (g s) -> p g s", g=2))
                        e.drain()
                        e.tensor_scalar_add(qvh[:], mvh[:, 1:2], EPS)
                        e.drain()
                        return e.reciprocal(qvh[:], qvh[:])
                    v_x1h = st("vector", [t_owph, d_xh, d_mrh], f, drain=True)

                for hp in range(NHP):
                    d_rz = st("gpsimd", [d_r1, t_r1_user[hp % 2]],
                              lambda e, hp=hp: e.dma_start(
                                  out=rzrow[64:65, hp % 2, :, :].rearrange(
                                      "o a b -> o (a b)"),
                                  in_=rr_d[0:1, 1024 * hp:1024 * (hp + 1)]))

                    def f(e, hp=hp):
                        e.matmul(PSC[0:64, :], ones64p[64:65, 0:64],
                                 rzrow[64:65, hp % 2, 0, :], start=True, stop=True)
                        return e.matmul(PSC[64:128, :], ones64p[64:65, 0:64],
                                        rzrow[64:65, hp % 2, 1, :], start=True, stop=True,
                                        tile_position=(64, 64))
                    t_r1 = st("tensor", [d_rz, d_obs[hp % 2], v_norm_prev,
                                         v_x1h if hp == 0 else None], f)
                    t_r1_user[hp % 2] = t_r1

                    def f(e, hp=hp):
                        e.tensor_mul(ofm[0:64, hp, :], ofm[0:64, hp, :],
                                     PSC[0:64, :])
                        return e.tensor_mul(ofm[64:128, hp, :], ofm[64:128, hp, :],
                                            PSC[64:128, :])
                    v_norm_prev = st("vector", [t_r1], f)

                if probe:
                    d_dump_prev = st("gpsimd", [v_norm_prev],
                                     lambda e, qg=qg: e.dma_start(
                                         out=ofm_dump[qg, :, :, :], in_=ofm[:]))

                d_wrt = st("sync", pa_dG + [t_w_user[1]],
                           lambda e: e.dma_start(out=wwin[:, 1, :, :],
                                                 in_=w_scr["wrt"][:, :, :]))

                v_x1 = [None] * 4
                t_owp = [None] * 4
                for t4 in range(4):
                    dst = (S_A, S_B)[t4 % 2]

                    def f(e, t4=t4, dst=dst):
                        for n in range(2):
                            for vc in range(NCH):
                                r = e.matmul(dst[:, 512 * n:512 * (n + 1)],
                                             ofm[:, vc, 128 * t4:128 * (t4 + 1)],
                                             wwin[:, 0, vc, 512 * n:512 * (n + 1)],
                                             start=(vc == 0), stop=(vc == NCH - 1))
                        return r
                    t_owp[t4] = st("tensor", [v_norm_prev, d_wp,
                                              v_x1h if t4 == 0 else None] +
                                   ([v_x1[t4 - 2]] if t4 >= 2 else []), f)

                    d_x2 = st("sync", [v_x1[t4 - 2] if t4 >= 2 else None,
                                       d_y_prev[t4 % 2]],
                              lambda e, qg=qg, t4=t4: e.dma_start(
                                  out=xt2[:, t4 % 2, :], in_=x_t[4 * qg + t4, :, :]))

                    def f(e, qg=qg, t4=t4, dst=dst):
                        tc = 4 * qg + t4
                        e.tensor_scalar(xhf2[:], xt2[:, t4 % 2, :], mfc[:, tc:tc + 1],
                                        None, ALU.subtract)
                        e.drain()
                        e.tensor_scalar_mul(xhf2[:], xhf2[:], rfc[:, tc:tc + 1])
                        e.drain()
                        e.tensor_mul(xhf2[:], xhf2[:], ga1t[:])
                        e.drain()
                        e.tensor_add(xhf2[:], xhf2[:], ga0t[:])
                        e.drain()
                        e.tensor_add(xhf2[:], xhf2[:], xt2[:, t4 % 2, :])
                        e.drain()
                        e.tensor_add(x12[:, t4, :], xhf2[:], dst[:, :])
                        e.drain()
                        e.bn_stats(bn12b[:, 0:6], x12[:, t4, 0:512])
                        e.bn_stats(bn12b[:, 6:12], x12[:, t4, 512:1024])
                        e.drain()
                        e.bn_aggr(mv2b[:], bn12b[:].rearrange("p (g s) -> p g s", g=2))
                        e.drain()
                        e.tensor_copy(m2c[:, t4:t4 + 1], mv2b[:, 0:1])
                        e.tensor_scalar_add(w2c[:, t4:t4 + 1], mv2b[:, 1:2], EPS)
                        e.drain()
                        return e.reciprocal(r2c[:, t4:t4 + 1], w2c[:, t4:t4 + 1])
                    v_x1[t4] = st("vector", [t_owp[t4], d_x2], f, drain=True)

                def f(e, qg=qg):
                    r = e.activation(r2c[:, :], r2c[:, :], AF.Sqrt)
                    if qg == 0:
                        r = e.activation(rsh[:, :], qvh[:, :], AF.Sqrt)
                    return r
                s_r2 = st("scalar", [v_x1[3], v_x1h], f, drain=True)

                if qg == 0:
                    def f(e):
                        e.tensor_scalar(xhf2[0:1, :], x1h, mvh[:, 0:1], None,
                                        ALU.subtract)
                        e.drain()
                        return e.tensor_scalar_mul(xh2hr, xhf2[0:1, :], rsh[:, 0:1])
                    v_xh2h = st("vector", [s_r2], f, drain=True)

                    def f(e):
                        for j in range(NCH):
                            r = e.matmul(PSC[:, j:j + 1], xh2hr[0:1, 128 * j:128 * (j + 1)],
                                         one11b[0:1, 0:1], start=True, stop=True)
                        return r
                    t_bch = st("tensor", [v_xh2h, v_norm_prev], f)
                    v_bch = st("vector", [t_bch],
                               lambda e: e.tensor_copy(bcolH[:, :], PSC[:, 0:8]))

                    def f(e):
                        e.tensor_scalar_mul(tmp8[:], bcolH[:], psv[:, 0:1])
                        e.drain()
                        return e.tensor_add(
                            x2fm[:, :, 0:1].rearrange("p i o -> p (i o)"),
                            tmp8[:], zterm[:, :])
                    v_bc = st("vector", [v_bch, v_zterm, v_x1[3]], f, drain=True)
                else:
                    v_bc = st("vector", [v_x2fm_prev, v_x1[3]],
                              lambda e: e.tensor_copy(
                                  x2fm[:, :, 0:1].rearrange("p i o -> p (i o)"),
                                  bcol[:, :]), drain=True)

                v_tr2 = [None] * 4
                t_tr2_prev = None
                for t4 in range(4):
                    def f(e, t4=t4):
                        e.tensor_scalar(xhf2[:], x12[:, t4, :], m2c[:, t4:t4 + 1],
                                        None, ALU.subtract)
                        e.drain()
                        return e.tensor_scalar_mul(xh2b[:], xhf2[:], r2c[:, t4:t4 + 1])
                    v_xh2 = st("vector", [s_r2, v_bc, v_tr2[t4 - 1] if t4 else None], f,
                               drain=True)

                    def f(e):
                        for j in range(NCH):
                            r = e.transpose(PSCB[:, 128 * j:128 * (j + 1)],
                                            xh2b[:, 128 * j:128 * (j + 1)], identb[:])
                        return r
                    t_tr2 = st("tensor", [v_xh2, t_tr2_prev], f)
                    t_tr2_prev = t_tr2
                    v_tr2[t4] = st("vector", [t_tr2],
                                   lambda e, t4=t4: e.tensor_copy(
                                       x2fm[:, :, 1 + 128 * t4:1 + 128 * (t4 + 1)],
                                       PSCB[:, :].rearrange("p (i s) -> p i s", i=8)))

                v_x2fm_prev = st("vector", [v_tr2[3]],
                                 lambda e: e.tensor_copy(
                                     bcol[:, :],
                                     x2fm[:, :, 512:513].rearrange("p i o -> p (i o)")),
                                 drain=True)

                def f(e):
                    e.tensor_sub(dmix[:, :, :], x2fm[:, :, 1:513], x2fm[:, :, 0:512])
                    e.drain()
                    for i in range(NCH):
                        r = e.scalar_tensor_tensor(xmix[:, i, :], dmix[:, i, :],
                                                   muR[:, i:i + 1], x2fm[:, i, 0:512],
                                                   ALU.mult, ALU.add)
                    return r
                v_mix = st("vector", [v_tr2[3], v_x2fm_prev, d_small], f,
                               drain=True)

                s_tanh = [None] * NCH
                t_rmm_last = None
                for j in range(NCH):
                    po = (PO_A, PO_B)[j % 2]

                    def f(e, j=j, po=po):
                        for i in range(NCH):
                            r = e.matmul(po[:, :], wwin[:, 1, i, 128 * j:128 * (j + 1)],
                                         xmix[:, i, :], start=(i == 0), stop=(i == NCH - 1))
                        return r
                    t_rmm = st("tensor", [v_mix, d_wrt] +
                               ([s_tanh[j - 2]] if j >= 2 else []), f)
                    t_rmm_last = t_rmm
                    s_tanh[j] = st("scalar", [t_rmm],
                                   lambda e, j=j, po=po: e.activation(
                                       tbuf[:, j, :], po[:, :], AF.Tanh,
                                       bias=brh_f[:, j:j + 1], scale=0.5),
                                   drain=(j == 0))

                def f(e):
                    for i in range(NCH):
                        r = e.scalar_tensor_tensor(xmix[:, i, :], dmix[:, i, :],
                                                   muK[:, i:i + 1], x2fm[:, i, 0:512],
                                                   ALU.mult, ALU.add)
                    return r
                v_mixk = st("vector", [t_rmm_last], f, drain=True)

                d_wkt = st("sync", [t_owp[3]],
                           lambda e: e.dma_start(out=wwin[:, 0, :, :],
                                                 in_=w_scr["wkt"][:, :, :]))
                s_kr = [None] * NCH
                v_ka = [None] * NCH
                t_kmm_last = None
                for j in range(NCH):
                    po = (PO_A, PO_B)[j % 2]

                    def f(e, j=j, po=po):
                        for i in range(NCH):
                            r = e.matmul(po[:, :], wwin[:, 0, i, 128 * j:128 * (j + 1)],
                                         xmix[:, i, :], start=(i == 0), stop=(i == NCH - 1))
                        return r
                    t_kmm = st("tensor", [v_mixk, d_wkt,
                                          s_tanh[NCH - 1] if j == 0 else None] +
                               ([v_ka[j - 2]] if j >= 2 else []), f)
                    t_kmm_last = t_kmm
                    s_kr[j] = st("scalar", [t_kmm, v_ka[j - 4] if j >= 4 else None],
                                 lambda e, j=j, po=po: e.activation(
                                     krb[:, j % 4, :], po[:, :], AF.Relu,
                                     bias=bk2_f[:, j:j + 1], scale=1.0),
                                 drain=(j % 4 == 0))
                    v_ka[j] = st("vector", [s_kr[j]],
                                 lambda e, j=j: e.tensor_mul(kab[:, j, :],
                                                             krb[:, j % 4, :],
                                                             krb[:, j % 4, :]))

                d_wvt = st("sync", [t_rmm_last],
                           lambda e: e.dma_start(out=wwin[:, 1, :, :],
                                                 in_=w_scr["wvt"][:, :, :]))
                v_ff = [None] * NCH
                t_vmm_last = None
                for j in range(NCH):
                    po = (PO_A, PO_B)[j % 2]

                    def f(e, j=j, po=po):
                        for i in range(NCH):
                            r = e.matmul(po[:, :], wwin[:, 1, i, 128 * j:128 * (j + 1)],
                                         kab[:, i, :], start=(i == 0), stop=(i == NCH - 1))
                        return r
                    t_vmm = st("tensor", [v_ka[NCH - 1], d_wvt,
                                          s_kr[NCH - 1] if j == 0 else None] +
                               ([v_ff[j - 2]] if j >= 2 else []), f)
                    t_vmm_last = t_vmm
                    v_ff[j] = st("vector", [t_vmm],
                                 lambda e, j=j, po=po: e.scalar_tensor_tensor(
                                     ffb[:, j, :], tbuf[:, j, :], 1.0, po[:, :],
                                     ALU.add, ALU.mult),
                                 drain=(j == 0))
                t_w_user[0] = t_kmm_last
                t_w_user[1] = t_vmm_last
                t_qwin_user = t_vmm_last

                t_ftr_prev = None
                v_x2_prev = None
                for t4 in range(4):
                    def f(e, t4=t4):
                        for j in range(NCH):
                            r = e.transpose(PSCB[:, 128 * j:128 * (j + 1)],
                                            ffb[:, j, 128 * t4:128 * (t4 + 1)], identb[:])
                        return r
                    t_ftr = st("tensor", v_ff + [t_ftr_prev, v_x2_prev], f)

                    def f(e, t4=t4):
                        e.tensor_copy(ff32[:], PSCB[:, :])
                        e.drain()
                        return e.tensor_add(x2o[:, t4 % 2, :], x12[:, t4, :], ff32[:])
                    v_x2 = st("vector", [t_ftr, d_y_prev[t4 % 2]], f, drain=True)
                    t_ftr_prev = t_ftr
                    v_x2_prev = v_x2
                    d_y = st("sync", [v_x2],
                             lambda e, qg=qg, t4=t4: e.dma_start(
                                 out=y_t[4 * qg + t4, :, :], in_=x2o[:, t4 % 2, :]))
                    d_y_prev[t4 % 2] = d_y
                    d_y_all.append(d_y)

                prev_qg_last_v = v_x2_prev

            st("gpsimd_c", [d_y_all[-1], d_y_all[-2]], lambda e: e.engine_nop())
            pr.emit(blk)

    return nc


_BF = ml_dtypes.bfloat16


def kernel(**inputs):
    x = np.asarray(inputs["L0_attn_x"], dtype=np.float32)
    Wq = np.ascontiguousarray(np.asarray(inputs["L0_attn_Wq"], dtype=np.float32))
    Wk = np.ascontiguousarray(np.asarray(inputs["L0_attn_Wk"], dtype=np.float32))
    Wv = np.ascontiguousarray(np.asarray(inputs["L0_attn_Wv"], dtype=np.float32))
    Wp = np.ascontiguousarray(np.asarray(inputs["L0_attn_Wp"], dtype=np.float32))
    WrT = np.ascontiguousarray(np.asarray(inputs["L0_ffn_W_r"], dtype=np.float32).T)
    WkT = np.ascontiguousarray(np.asarray(inputs["L0_ffn_W_k"], dtype=np.float32).T)
    WvT = np.ascontiguousarray(np.asarray(inputs["L0_ffn_W_v"], dtype=np.float32).T)
    Wmod = np.ascontiguousarray(np.asarray(inputs["L0_wmod"], dtype=np.float32))
    mu_r = np.asarray(inputs["L0_ffn_mu_r"], dtype=np.float32)
    mu_k = np.asarray(inputs["L0_ffn_mu_k"], dtype=np.float32)
    Lc = np.asarray(inputs["L0_c"], dtype=np.float32)

    murfm = np.ascontiguousarray(mu_r.reshape(8, 128).T)
    mukfm = np.ascontiguousarray(mu_k.reshape(8, 128).T)
    ident = np.eye(128, dtype=_BF)
    identf = np.eye(128, dtype=np.float32)

    nc = build_nc()
    in_maps = []
    for c in range(8):
        b, half = c // 2, c % 2
        t0 = half * TOWN
        xr = np.ascontiguousarray(np.roll(x[b], -t0, axis=0))
        in_maps.append({
            "xfull": xr, "wq": Wq, "wk": Wk, "wv": Wv, "wp": Wp,
            "wrt": WrT, "wkt": WkT, "wvt": WvT, "wmod": Wmod,
            "c_fm": np.ascontiguousarray(Lc[b].reshape(8, 128).T),
            "mu_r_fm": murfm, "mu_k_fm": mukfm,
            "ps_col": np.full((128, 1), float(half), np.float32),
            "identb": ident, "identf": identf,
        })
    res = run_bass_kernel_spmd(nc, in_maps, core_ids=list(range(8)))
    out = np.empty((4, SEQ, D), np.float32)
    for c in range(8):
        b, half = c // 2, c % 2
        out[b, half * TOWN:(half + 1) * TOWN] = np.asarray(res.results[c]["y"])
    return out
